# version 15
# baseline (speedup 1.0000x reference)
import numpy as np
import concourse.bass as bass
import concourse.mybir as mybir
from concourse.bass_utils import run_bass_kernel_spmd

F32 = mybir.dt.float32
BF16 = mybir.dt.bfloat16
AF = mybir.ActivationFunctionType
ALU = mybir.AluOpType
AX = mybir.AxisListType

D = 1024
S = 4096
HN = 32
FF = 4096
T = 256
NT = T // 128
NCH = S // T
KD = D // 128
KH = FF // 128
EPS = 1e-5


class Emitter:
    def __init__(self, nc):
        self.nc = nc
        self.ops = {k: [] for k in ("sync", "gpsimd", "vector", "scalar", "tensor")}
        self.cnt = {k: 0 for k in ("ld", "wl", "wd", "we", "w2", "w3", "w4", "w5", "w6", "w7", "le", "lf", "st", "su", "sv", "v", "s", "t", "g")}
        self.sems = {k: nc.alloc_semaphore("sem_" + k) for k in self.cnt}

    def op(self, eng, fn, inc=None, inc_n=1, waits=()):
        for sname, val in waits:
            assert val <= self.cnt[sname], (eng, sname, val, self.cnt[sname])
        if inc is not None:
            self.cnt[inc] += inc_n
        sems = self.sems
        isem = sems[inc] if inc is not None else None

        def run(e, fn=fn, waits=tuple(waits), isem=isem, n=inc_n):
            for sname, val in waits:
                e.wait_ge(sems[sname], val)
            ins = fn(e)
            if isem is not None:
                ins.then_inc(isem, n)

        self.ops[eng].append(run)
        return self.cnt[inc] if inc is not None else None

    def selfsync(self, eng, sem):
        val = self.cnt[sem]
        sems = self.sems

        def run(e, val=val, sem=sem):
            e.wait_ge(sems[sem], val)

        self.ops[eng].append(run)


def build():
    nc = bass.Bass("TRN2", target_bir_lowering=False)

    x_d = nc.dram_tensor("x", [S, D], F32, kind="ExternalInput")
    wa_d = nc.dram_tensor("wa", [HN, D], F32, kind="ExternalInput")
    wi_d = nc.dram_tensor("wi", [HN, D], F32, kind="ExternalInput")
    wo_d = nc.dram_tensor("wo", [D, HN], F32, kind="ExternalInput")
    ba_d = nc.dram_tensor("ba", [1, HN], F32, kind="ExternalInput")
    bi_d = nc.dram_tensor("bi", [1, HN], F32, kind="ExternalInput")
    bo_d = nc.dram_tensor("bo", [1, D], F32, kind="ExternalInput")
    gm_d = nc.dram_tensor("gm", [1, D], F32, kind="ExternalInput")
    bt_d = nc.dram_tensor("bt", [1, D], F32, kind="ExternalInput")
    w1_d = nc.dram_tensor("w1", [FF, D], F32, kind="ExternalInput")
    b1_d = nc.dram_tensor("b1v", [KH, 128], F32, kind="ExternalInput")
    w2_d = nc.dram_tensor("w2", [D, FF], F32, kind="ExternalInput")
    b2_d = nc.dram_tensor("b2v", [1, D], F32, kind="ExternalInput")
    out_d = nc.dram_tensor("out", [S, D], F32, kind="ExternalOutput")

    w1T = nc.alloc_sbuf_tensor("w1T", [128, KD, KH, 128], BF16)
    w2T = nc.alloc_sbuf_tensor("w2T", [128, KH, 1024], BF16)
    waT_bf = nc.alloc_sbuf_tensor("waT_bf", [128, KD, HN], BF16)
    winT_bf = nc.alloc_sbuf_tensor("winT_bf", [128, KD, HN], BF16)
    woutT_bf = nc.alloc_sbuf_tensor("woutT_bf", [HN + 1, D], BF16)
    gamma_f = nc.alloc_sbuf_tensor("gamma_f", [128, KD], F32)
    beta_f = nc.alloc_sbuf_tensor("beta_f", [128, KD], F32)
    b1_col = nc.alloc_sbuf_tensor("b1_col", [128, KH], F32)
    ba_col = nc.alloc_sbuf_tensor("ba_col", [HN, 2], F32)
    bab_p = nc.alloc_sbuf_tensor("bab_p", [HN, 2], F32)
    b2_bc = nc.alloc_sbuf_tensor("b2_bc", [128, D], F32)
    ident_b = nc.alloc_sbuf_tensor("ident_b", [128, 128], BF16)
    ones_f = nc.alloc_sbuf_tensor("ones_f", [1, 128], F32)

    x_sbA = nc.alloc_sbuf_tensor("x_sbA", [128, NT, D], F32)
    x_sbB = nc.alloc_sbuf_tensor("x_sbB", [128, NT, D], F32)
    x_sbC = nc.alloc_sbuf_tensor("x_sbC", [128, NT, D], F32)
    xnf_bfA = nc.alloc_sbuf_tensor("xnf_bfA", [128, KD, T], BF16)
    xnf_bfB = nc.alloc_sbuf_tensor("xnf_bfB", [128, KD, T], BF16)
    uf_bf = nc.alloc_sbuf_tensor("uf_bf", [128, KD, T], BF16)
    h1_bf = nc.alloc_sbuf_tensor("h1_bf", [128, KH, T], BF16)
    g_sb = nc.alloc_sbuf_tensor("g_sb", [128, D], F32)
    s_sb = nc.alloc_sbuf_tensor("s_sb", [128, D], F32)
    stage = nc.alloc_sbuf_tensor("stage", [128, 4 * D], BF16)
    a_s = nc.alloc_sbuf_tensor("a_s", [HN, T], F32)
    b_s = nc.alloc_sbuf_tensor("b_s", [HN, T], F32)
    h_s = nc.alloc_sbuf_tensor("h_s", [HN, T], F32)
    h_bf = nc.alloc_sbuf_tensor("h_bf", [HN + 1, T], BF16)
    carry = nc.alloc_sbuf_tensor("carry", [HN, 1], F32)
    st6 = nc.alloc_sbuf_tensor("st6", [128, 12], F32)
    mv_all = nc.alloc_sbuf_tensor("mv_all", [128, 4, NT, 2], F32)
    r1_all = nc.alloc_sbuf_tensor("r1_all", [128, 4, NT, 1], F32)
    r12_all = nc.alloc_sbuf_tensor("r12_all", [128, NT], F32)
    veps = nc.alloc_sbuf_tensor("veps", [128, NT], F32)
    stdt = nc.alloc_sbuf_tensor("stdt", [128, 2 * NT], F32)
    w2t = nc.alloc_sbuf_tensor("w2t", [128, NT], F32)
    pw = veps
    alph = nc.alloc_sbuf_tensor("alph", [128, 6], F32)

    h1_flat = h1_bf[:].rearrange("p a b -> p (a b)")
    wstg = lambda a, b: h1_flat[:, a:b]
    x_flat = x_sbA[:].rearrange("p a b -> p (a b)")
    wa_st = lambda lo, j0, j1: x_flat[lo:lo + HN, j0:j1]
    waT_f = s_sb[:][:, 0:256].rearrange("p (a b) -> p a b", b=HN)
    winT_f = s_sb[:][:, 256:512].rearrange("p (a b) -> p a b", b=HN)
    wo_sb = g_sb[:][:, 0:256].rearrange("p (a b) -> p a b", b=HN)
    xB_flat = x_sbB[:].rearrange("p a b -> p (a b)")
    rows_sb = xB_flat[0:1, 0:D]
    xC_flat = x_sbC[:].rearrange("p a b -> p (a b)")
    ident_f = xC_flat[:, 0:128]
    b1v_sb = xC_flat[0:HN, 128:256]
    bab_sb = xC_flat[0:1, 256:256 + 2 * HN]

    ps_tr0 = nc.alloc_psum_tensor("ps_tr0", [128, 1024], BF16)
    ps_tr1 = nc.alloc_psum_tensor("ps_tr1", [128, 1024], BF16)
    ps_zab = nc.alloc_psum_tensor("ps_zab", [HN, 512], F32)
    ps_h1a = nc.alloc_psum_tensor("ps_h1a", [128, 512], F32)
    ps_h1b = nc.alloc_psum_tensor("ps_h1b", [128, 512], F32)
    ps_ga = nc.alloc_psum_tensor("ps_ga", [128, 512], F32)
    ps_gb = nc.alloc_psum_tensor("ps_gb", [128, 512], F32)
    ps_ms = nc.alloc_psum_tensor("ps_ms", [128, 512], F32)

    em = Emitter(nc)
    opx = em.op
    C = em.cnt

    opx("gpsimd", lambda e: e.memset(ident_f, 0.0), inc="g")
    g_if0 = C["g"]
    opx("gpsimd", lambda e: e.affine_select(
        out=ident_f, in_=ident_f, compare_op=ALU.not_equal, fill=1.0,
        base=0, pattern=[[-1, 128]], channel_multiplier=1),
        inc="g", waits=[("g", g_if0)])
    opx("gpsimd", lambda e: e.memset(ident_b[:], 0.0), inc="g")
    g_ib0 = C["g"]
    opx("gpsimd", lambda e: e.affine_select(
        out=ident_b[:], in_=ident_b[:], compare_op=ALU.not_equal, fill=1.0,
        base=0, pattern=[[-1, 128]], channel_multiplier=1),
        inc="g", waits=[("g", g_ib0)])
    opx("gpsimd", lambda e: e.memset(ones_f[:], 1.0), inc="g")
    opx("gpsimd", lambda e: e.memset(h_bf[HN:HN + 1, :], 1.0), inc="g")
    g_setup = C["g"]

    opx("gpsimd", lambda e: e.dma_start(wa_st(0, 0, D), wa_d[:]), inc="wl", inc_n=16)
    opx("gpsimd", lambda e: e.dma_start(
        wo_sb, wo_d[:].rearrange("(a p) c -> p a c", p=128)), inc="wl", inc_n=16)
    opx("gpsimd", lambda e: e.dma_start(bab_sb[0:1, 0:HN], ba_d[:]), inc="wl", inc_n=16)
    opx("gpsimd", lambda e: e.dma_start(bab_sb[0:1, HN:2 * HN], bi_d[:]), inc="wl", inc_n=16)
    opx("gpsimd", lambda e: e.dma_start(rows_sb, gm_d[:]), inc="wl", inc_n=16)
    opx("gpsimd", lambda e: e.dma_start(b1v_sb, b1_d[:]), inc="wl", inc_n=16)
    opx("gpsimd", lambda e: e.dma_start(woutT_bf[HN:HN + 1, :], bo_d[:]), inc="wl", inc_n=16)
    wl_small = C["wl"]

    opx("vector", lambda e: e.memset(carry[:], 0.0), inc="v")

    for j in range(KD):
        opx("tensor", lambda e, j=j: e.transpose(
            ps_ms[:, j * HN:(j + 1) * HN], wa_st(0, j * 128, (j + 1) * 128),
            ident_f[0:HN, 0:HN]),
            waits=[("wl", wl_small), ("g", g_setup)] if j == 0 else (),
            inc="t" if j == KD - 1 else None)
    t_r1a = C["t"]
    opx("vector", lambda e: e.tensor_copy(
        waT_f, ps_ms[:, 0:256].rearrange("p (a b) -> p a b", b=HN)),
        waits=[("t", t_r1a)], inc="v")
    v_r1a = C["v"]
    opx("gpsimd", lambda e: e.dma_start(wa_st(0, 0, D), wi_d[:]),
        inc="wl", inc_n=16, waits=[("t", t_r1a)])
    wl_wi = C["wl"]
    for j in range(KD):
        opx("tensor", lambda e, j=j: e.transpose(
            ps_ms[:, 256 + j * HN:256 + (j + 1) * HN],
            wa_st(0, j * 128, (j + 1) * 128), ident_f[0:HN, 0:HN]),
            waits=[("wl", wl_wi)] if j == 0 else (),
            inc="t" if j == KD - 1 else None)
    t_r1 = C["t"]
    opx("vector", lambda e: e.tensor_copy(
        winT_f, ps_ms[:, 256:512].rearrange("p (a b) -> p a b", b=HN)),
        waits=[("t", t_r1)], inc="v")
    v_r1 = C["v"]

    for j in range(KD):
        opx("tensor", lambda e, j=j: e.transpose(
            ps_ms[:, j:j + 1], rows_sb[:, j * 128:(j + 1) * 128],
            ident_f[0:1, 0:1]),
            waits=[("v", v_r1)] if j == 0 else ())
    opx("tensor", lambda e: e.transpose(
        ps_ms[:, 16:48], b1v_sb, ident_f[0:HN, 0:HN]))
    opx("tensor", lambda e: e.transpose(
        ps_ms[0:HN, 48:49], bab_sb[0:1, 0:HN], ident_f[0:1, 0:1]))
    opx("tensor", lambda e: e.transpose(
        ps_ms[0:HN, 49:50], bab_sb[0:1, HN:2 * HN], ident_f[0:1, 0:1]), inc="t")
    t_r2a = C["t"]
    opx("vector", lambda e: e.tensor_copy(gamma_f[:], ps_ms[:, 0:8]),
        waits=[("t", t_r2a)])
    opx("vector", lambda e: e.tensor_copy(b1_col[:], ps_ms[:, 16:48]))
    opx("vector", lambda e: e.tensor_copy(ba_col[:], ps_ms[0:HN, 48:50]), inc="v")
    v_r2a = C["v"]
    em.selfsync("vector", "v")
    opx("gpsimd", lambda e: e.dma_start(rows_sb, bt_d[:]),
        inc="wl", inc_n=16, waits=[("t", t_r2a)])
    wl_bt = C["wl"]
    for j in range(KD):
        opx("tensor", lambda e, j=j: e.transpose(
            ps_ms[:, 8 + j:9 + j], rows_sb[:, j * 128:(j + 1) * 128],
            ident_f[0:1, 0:1]),
            waits=[("wl", wl_bt), ("v", v_r2a)] if j == 0 else (),
            inc="t" if j == KD - 1 else None)
    t_r2b = C["t"]
    opx("vector", lambda e: e.tensor_copy(beta_f[:], ps_ms[:, 8:16]),
        waits=[("t", t_r2b)], inc="v")
    v_r2 = C["v"]
    em.selfsync("vector", "v")
    opx("gpsimd", lambda e: e.dma_start(rows_sb, b2_d[:]),
        inc="wl", inc_n=16, waits=[("t", t_r2b)])
    wl_b2 = C["wl"]

    v_evac_r3 = v_r2
    for h in range(2):
        for j in range(4):
            a = h * 4 + j
            opx("tensor", lambda e, a=a, j=j: e.transpose(
                ps_ms[0:HN, j * 128:(j + 1) * 128], wo_sb[:, a, :], ident_f),
                waits=[("v", v_evac_r3)] if j == 0 else (),
                inc="t" if j == 3 else None)
        t_r3 = C["t"]
        opx("vector", lambda e, h=h: e.tensor_copy(
            woutT_bf[0:HN, h * 512:(h + 1) * 512], ps_ms[0:HN, 0:512]),
            waits=[("t", t_r3)], inc="v")
        v_evac_r3 = C["v"]
    v_r3 = C["v"]

    for j in range(KD):
        opx("vector", lambda e, j=j: e.tensor_scalar_mul(
            waT_bf[:, j, :], waT_f[:, j, :], gamma_f[:, j:j + 1]))
    for j in range(KD):
        opx("vector", lambda e, j=j: e.tensor_scalar_mul(
            winT_bf[:, j, :], winT_f[:, j, :], gamma_f[:, j:j + 1]),
            inc="v" if j == KD - 1 else None)
    v_r4 = C["v"]

    for j in range(KD):
        opx("tensor", lambda e, j=j: e.matmul(
            ps_ms[0:HN, 0:1], waT_f[:, j, :], beta_f[:, j:j + 1],
            start=(j == 0), stop=(j == KD - 1)),
            waits=[("v", v_r3)] if j == 0 else ())
    for j in range(KD):
        opx("tensor", lambda e, j=j: e.matmul(
            ps_ms[0:HN, 1:2], winT_f[:, j, :], beta_f[:, j:j + 1],
            start=(j == 0), stop=(j == KD - 1)),
            inc="t" if j == KD - 1 else None)
    t_r5 = C["t"]
    opx("vector", lambda e: e.tensor_add(bab_p[:], ps_ms[0:HN, 0:2], ba_col[:]),
        waits=[("t", t_r5)], inc="v")
    v_r5 = C["v"]

    opx("tensor", lambda e: e.matmul(
        ps_ms[:, 0:512], ones_f[0:1, :], rows_sb[:, 0:512],
        start=True, stop=True),
        waits=[("v", v_r5), ("wl", wl_b2)], inc="t")
    t_b2a = C["t"]
    opx("vector", lambda e: e.tensor_copy(b2_bc[:, 0:512], ps_ms[:, 0:512]),
        waits=[("t", t_b2a)], inc="v")
    v_b2a = C["v"]
    opx("tensor", lambda e: e.matmul(
        ps_ms[:, 0:512], ones_f[0:1, :], rows_sb[:, 512:1024],
        start=True, stop=True),
        waits=[("v", v_b2a)], inc="t")
    t_b2b = C["t"]
    opx("vector", lambda e: e.tensor_copy(b2_bc[:, 512:1024], ps_ms[:, 0:512]),
        waits=[("t", t_b2b)], inc="v")
    v_b2_done = C["v"]


    ps_tr = [ps_tr0, ps_tr1]
    tr_free = {0: ("v", 0), 1: ("v", 0)}
    tr_fill = {0: 0, 1: 0}

    def trans8(bank, src_slices, dst_ap_fn, extra_waits, evac_eng="vector"):
        ps = ps_tr[bank]
        wts = list(extra_waits) + [tr_free[bank]]
        for k, src in enumerate(src_slices):
            opx("tensor", lambda e, k=k, src=src, ps=ps: e.transpose(
                ps[:, k * 128:(k + 1) * 128], src, ident_b[:]),
                waits=wts if k == 0 else (), inc="t" if k == len(src_slices) - 1 else None)
        tr_fill[bank] = C["t"]
        if evac_eng == "vector":
            opx("vector", lambda e, ps=ps: e.tensor_copy(
                dst_ap_fn(), ps[:].rearrange("p (a b) -> p a b", b=128)),
                waits=[("t", tr_fill[bank])], inc="v")
            tr_free[bank] = ("v", C["v"])
        else:
            opx("scalar", lambda e, ps=ps: e.copy(
                dst_ap_fn(), ps[:].rearrange("p (a b) -> p a b", b=128)),
                waits=[("t", tr_fill[bank])], inc="s")
            tr_free[bank] = ("s", C["s"])
        return tr_fill[bank]

    w1_fill = [0] * KH
    wsems = ["wd", "we", "w2", "w3", "w4", "w5", "w6", "w7"]

    def w1_part(i):
        o = (i % 8) * 1024
        a = (i % 3) * 2
        psem = wsems[i % 8]
        wl_i = C[psem]
        opx("vector", lambda e, o=o, a=a: e.reduce_sum(
            alph[:, a:a + 1], wstg(o, o + 1024), AX.X, apply_absolute_value=True),
            waits=[(psem, wl_i)], inc="v")
        em.selfsync("vector", "v")
        opx("vector", lambda e, a=a: e.tensor_scalar_mul(
            alph[:, a + 1:a + 2], alph[:, a:a + 1], 1.0 / D), inc="v")
        v_alpha = C["v"]
        em.selfsync("vector", "v")
        opx("scalar", lambda e, o=o: e.sign(wstg(o, o + 1024), wstg(o, o + 1024)),
            waits=[("v", v_alpha)], inc="s")
        return C["s"]

    def w1_tail(i, s_sign):
        o = (i % 8) * 1024
        a = (i % 3) * 2
        opx("vector", lambda e, o=o, a=a: e.tensor_scalar_mul(
            wstg(o, o + 1024), wstg(o, o + 1024), alph[:, a + 1:a + 2]),
            waits=[("s", s_sign)], inc="v")
        v_fold = C["v"]
        srcs = [wstg(o + k * 128, o + (k + 1) * 128) for k in range(KD)]
        w1_fill[i] = trans8(
            i % 2, srcs, (lambda i=i: w1T[:, :, i, :]), [("v", v_fold)],
            evac_eng="scalar" if i % 2 == 1 else "vector")

    w1_sign = {}
    for i in range(KH):
        wts = [("t", w1_fill[i - 8])] if i >= 8 else []
        psem = wsems[i % 8]
        opx("gpsimd", lambda e, i=i: e.dma_start(
            wstg((i % 8) * 1024, (i % 8) * 1024 + 1024),
            w1_d[i * 128:(i + 1) * 128, :]),
            inc=psem, inc_n=16, waits=wts)
        w1_sign[i] = w1_part(i)
        if i >= 1:
            w1_tail(i - 1, w1_sign[i - 1])
    w1_tail(KH - 1, w1_sign[KH - 1])

    w2_fill = [0] * KD
    w2_sign = {}

    def w2_round(i, r):
        o = (i % 2) * 4096
        srcs = [wstg(o + (r * 8 + j) * 128, o + (r * 8 + j + 1) * 128)
                for j in range(8)]
        w2_fill[i] = trans8(
            r % 2, srcs,
            (lambda i=i, r=r: w2T[:, r * 8:(r + 1) * 8, i * 128:(i + 1) * 128]),
            [("v", w2_fold[i])] if r == 0 else [],
            evac_eng="scalar")

    w2_fold = {}
    for i in range(KD):
        o = (i % 2) * 4096
        if i == 0:
            wts = [("t", w1_fill[27])]
        elif i == 1:
            wts = [("t", w1_fill[31])]
        else:
            wts = [("t", w2_fill[i - 2])]
        psem = wsems[i % 2]
        opx("gpsimd", lambda e, i=i, o=o: e.dma_start(
            wstg(o, o + 4096), w2_d[i * 128:(i + 1) * 128, :]),
            inc=psem, inc_n=16, waits=wts)
        wl_i = C[psem]
        for r in range(4):
            opx("vector", lambda e, o=o, r=r: e.reduce_sum(
                stdt[:, r:r + 1], wstg(o + r * 1024, o + (r + 1) * 1024),
                AX.X, apply_absolute_value=True),
                waits=[(psem, wl_i)] if r == 0 else (),
                inc="v" if r == 3 else None)
            if i >= 1:
                w2_round(i - 1, r)
        em.selfsync("vector", "v")
        a = (i % 3) * 2
        opx("vector", lambda e, a=a: e.tensor_add(
            alph[:, a:a + 1], stdt[:, 0:1], stdt[:, 1:2]))
        opx("vector", lambda e: e.tensor_add(
            stdt[:, 2:3], stdt[:, 2:3], stdt[:, 3:4]), inc="v")
        em.selfsync("vector", "v")
        opx("vector", lambda e, a=a: e.tensor_add(
            alph[:, a:a + 1], alph[:, a:a + 1], stdt[:, 2:3]), inc="v")
        em.selfsync("vector", "v")
        opx("vector", lambda e, a=a: e.tensor_scalar_mul(
            alph[:, a + 1:a + 2], alph[:, a:a + 1], 1.0 / FF), inc="v")
        v_alpha = C["v"]
        em.selfsync("vector", "v")
        opx("scalar", lambda e, o=o: e.sign(wstg(o, o + 4096), wstg(o, o + 4096)),
            waits=[("v", v_alpha)], inc="s")
        s_sign = C["s"]
        opx("vector", lambda e, o=o, a=a: e.tensor_scalar_mul(
            wstg(o, o + 4096), wstg(o, o + 4096), alph[:, a + 1:a + 2]),
            waits=[("s", s_sign)], inc="v")
        w2_fold[i] = C["v"]
    for r in range(4):
        w2_round(KD - 1, r)

    prologue_end = dict(C)

    x_par = [x_sbA, x_sbB, x_sbC]
    ldsem = ["ld", "le", "lf"]
    xload_cnt = {}
    stage_free = {j: 0 for j in range(4)}
    PP = dict(vU=[0, 0], vXN=[0, 0], tZ=0, sZ=0, vB=0, vScan=0,
              gelu_tail=[0, 0, 0], g_free=0, lnS=0, uf_ev=("v", 0),
              xnf_ev=("v", 0), vS_last=0)
    xnf_par = [xnf_bfA, xnf_bfB]

    def emit_trans_group(cc, slot):
        t = slot // 2
        is_xn = slot % 2
        pcc = cc % 2
        if is_xn:
            stage_free[slot] = trans8(
                1, [stage[:, slot * D + k * 128:slot * D + (k + 1) * 128]
                    for k in range(KD)],
                (lambda t=t, pcc=pcc: xnf_par[pcc][:, :, t * 128:(t + 1) * 128]),
                [("v", PP["vXN"][t])])
            PP["xnf_ev"] = tr_free[1]
        else:
            stage_free[slot] = trans8(
                0, [stage[:, slot * D + k * 128:slot * D + (k + 1) * 128]
                    for k in range(KD)],
                (lambda t=t: uf_bf[:, :, t * 128:(t + 1) * 128]),
                [("v", PP["vU"][t])])
            PP["uf_ev"] = tr_free[0]
    gelu_done = [0] * KH

    def emit_xload(c):
        p = c % 3
        if c == 0:
            wts = [("t", t_r1)]
        elif c == 1:
            wts = [("t", t_b2b)]
        elif c == 2:
            wts = [("t", t_r3)]
        else:
            wts = [(("st", "su", "sv")[c % 3], 16 * ((c - 3) // 3 + 1))]
        opx("sync", lambda e, c=c, p=p: e.dma_start(
            x_par[p][:], x_d[c * T:(c + 1) * T, :].rearrange(
                "(t p) d -> p t d", p=128)),
            inc=ldsem[p], inc_n=16, waits=wts)
        xload_cnt[c] = C[ldsem[p]]

    def emit_ln_ph1(cc):
        p = cc % 4
        xt = x_par[cc % 3]
        vA = [0, 0]
        for t in range(NT):
            wts = []
            if t == 0:
                wts = [(ldsem[cc % 3], xload_cnt[cc]), ("s", PP["lnS"])]
            opx("vector", lambda e, t=t, xt=xt: e.bn_stats(
                st6[:, 0:6], xt[:, t, 0:512]), waits=wts)
            opx("vector", lambda e, t=t, xt=xt: e.bn_stats(
                st6[:, 6:12], xt[:, t, 512:1024]), inc="v")
            em.selfsync("vector", "v")
            opx("vector", lambda e, t=t, p=p: e.bn_aggr(
                mv_all[:, p, t, :], st6[:]), inc="v")
            em.selfsync("vector", "v")
            opx("vector", lambda e, t=t, p=p: e.tensor_scalar_add(
                veps[:, t:t + 1], mv_all[:, p, t, 1:2], EPS), inc="v")
            vA[t] = C["v"]
        return vA

    def emit_ln_sqrt1(cc, vA):
        sA = [0, 0]
        for t in range(NT):
            opx("scalar", lambda e, t=t: e.sqrt(
                stdt[:, t:t + 1], veps[:, t:t + 1]),
                waits=[("v", vA[t])], inc="s")
            sA[t] = C["s"]
        return sA

    def emit_ln_ph2(cc, sA):
        p = cc % 4
        vW = [0, 0]
        for t in range(NT):
            opx("vector", lambda e, t=t, p=p: e.reciprocal(
                r1_all[:, p, t, :], stdt[:, t:t + 1]),
                waits=[("s", sA[t])], inc="v")
            em.selfsync("vector", "v")
            opx("vector", lambda e, t=t, p=p: e.tensor_mul(
                pw[:, t:t + 1], r1_all[:, p, t, :], r1_all[:, p, t, :]), inc="v")
            em.selfsync("vector", "v")
            opx("vector", lambda e, t=t, p=p: e.tensor_scalar(
                w2t[:, t:t + 1], pw[:, t:t + 1], mv_all[:, p, t, 1:2], EPS,
                ALU.mult, ALU.add), inc="v")
            vW[t] = C["v"]
        return vW

    def emit_ln_sqrt2(cc, vW):
        sB = [0, 0]
        for t in range(NT):
            opx("scalar", lambda e, t=t: e.sqrt(
                stdt[:, NT + t:NT + t + 1], w2t[:, t:t + 1]),
                waits=[("v", vW[t])], inc="s")
            sB[t] = C["s"]
        PP["lnS"] = C["s"]
        return sB

    def emit_ln_ph3(cc, sB):
        p = cc % 4
        xt = x_par[cc % 3]
        for t in range(NT):
            opx("vector", lambda e, t=t: e.reciprocal(
                pw[:, t:t + 1], stdt[:, NT + t:NT + t + 1]),
                waits=[("s", sB[t])], inc="v")
            em.selfsync("vector", "v")
            opx("vector", lambda e, t=t, p=p: e.tensor_mul(
                r12_all[:, t:t + 1], r1_all[:, p, t, :], pw[:, t:t + 1]), inc="v")
            em.selfsync("vector", "v")
            opx("vector", lambda e, t=t, p=p, xt=xt: e.tensor_scalar(
                stage[:, (2 * t) * D:(2 * t + 1) * D], xt[:, t, :],
                mv_all[:, p, t, 0:1], r12_all[:, t:t + 1],
                ALU.subtract, ALU.mult),
                waits=[("t", stage_free[2 * t])], inc="v")
            PP["vU"][t] = C["v"]
            opx("vector", lambda e, t=t, p=p, xt=xt: e.tensor_scalar(
                stage[:, (2 * t + 1) * D:(2 * t + 2) * D], xt[:, t, :],
                mv_all[:, p, t, 0:1], r1_all[:, p, t, :],
                ALU.subtract, ALU.mult),
                waits=[("t", stage_free[2 * t + 1])], inc="v")
            PP["vXN"][t] = C["v"]

    def emit_ln_block(cc):
        vA = emit_ln_ph1(cc)
        sA = emit_ln_sqrt1(cc, vA)
        vW = emit_ln_ph2(cc, sA)
        sB = emit_ln_sqrt2(cc, vW)
        emit_ln_ph3(cc, sB)

    emit_xload(0)
    emit_xload(1)
    emit_ln_block(0)
    for slot in range(4):
        emit_trans_group(0, slot)
    emit_ln_block(1)

    for c in range(NCH):
        p = c % 3
        xt = x_par[p]
        pf = c % 2
        p4 = c % 4
        if c + 2 < NCH:
            emit_xload(c + 2)

        uf_ev = PP["uf_ev"]
        xnf_ev = PP["xnf_ev"]
        xnf_c = xnf_par[pf]

        for k in range(KD):
            opx("tensor", lambda e, k=k: e.matmul(
                ps_zab[:, 0:T], waT_bf[:, k, :], uf_bf[:, k, :],
                start=(k == 0), stop=(k == KD - 1)),
                waits=[uf_ev, ("s", PP["sZ"]), ("v", PP["vB"])]
                if k == 0 else ())
        for k in range(KD):
            opx("tensor", lambda e, k=k: e.matmul(
                ps_zab[:, 256:256 + T], winT_bf[:, k, :], uf_bf[:, k, :],
                start=(k == 0), stop=(k == KD - 1)),
                inc="t" if k == KD - 1 else None)
        tZ = C["t"]

        opx("scalar", lambda e: e.activation(
            a_s[:], ps_zab[:, 0:T], AF.Sigmoid, bias=bab_p[:, 0:1]),
            waits=[("t", tZ), ("v", PP["vScan"])], inc="s")
        sZ = C["s"]
        opx("scalar", lambda e: e.activation(alph[:, 0:1], alph[:, 0:1], AF.Gelu))
        opx("vector", lambda e: e.tensor_scalar_add(
            b_s[:], ps_zab[:, 256:256 + T], bab_p[:, 1:2]), waits=[("t", tZ)],
            inc="v")
        vB = C["v"]
        em.selfsync("vector", "v")
        opx("vector", lambda e: e.tensor_tensor_scan(
            h_s[:], a_s[:], b_s[:], carry[:, 0:1], ALU.mult, ALU.add),
            waits=[("s", sZ)], inc="v")
        vScan = C["v"]
        em.selfsync("vector", "v")
        opx("vector", lambda e: e.tensor_copy(carry[:], h_s[:, T - 1:T]), inc="v")
        em.selfsync("vector", "v")
        opx("vector", lambda e: e.tensor_copy(h_bf[0:HN, :], h_s[:]), inc="v")
        vH = C["v"]

        h1_banks = [ps_h1a, ps_h1b, ps_ms]
        for m in range(KH):
            if c + 1 < NCH and m in (13, 16, 19, 22):
                emit_trans_group(c + 1, (m - 13) // 3)
            ps = h1_banks[m % 3]
            if m == 0:
                wts = [xnf_ev, ("s", PP["gelu_tail"][0]),
                       ("v", PP["vS_last"])]
            elif m in (1, 2):
                wts = [("s", PP["gelu_tail"][m])]
                if m == 2 and c == 0:
                    wts.append(("v", v_b2_done))
            else:
                wts = [("s", gelu_done[m - 3])]
            for k in range(KD):
                opx("tensor", lambda e, k=k, m=m, ps=ps, xnf_c=xnf_c: e.matmul(
                    ps[:, 0:T], w1T[:, k, m, :], xnf_c[:, k, :],
                    start=(k == 0), stop=(k == KD - 1)),
                    waits=wts if k == 0 else (),
                    inc="t" if k == KD - 1 else None)
            tM = C["t"]
            opx("scalar", lambda e, m=m, ps=ps: e.activation(
                h1_bf[:, m, :], ps[:, 0:T], AF.Gelu, bias=b1_col[:, m:m + 1]),
                waits=[("t", tM)], inc="s")
            gelu_done[m] = C["s"]
        PP["gelu_tail"] = [gelu_done[30], gelu_done[31], gelu_done[29]]

        do_ln = c + 2 < NCH
        if do_ln:
            ln_vA = emit_ln_ph1(c + 2)
            ln_sA = emit_ln_sqrt1(c + 2, ln_vA)

        vS = 0
        for t in range(NT):
            awts = [("v", vH), ("s", gelu_done[KH - 1])] if t == 0 else [("v", vS)]
            opx("tensor", lambda e, t=t: e.matmul(
                ps_h1a[:, 0:512], h_bf[:, t * 128:(t + 1) * 128],
                woutT_bf[:, 0:512], start=True, stop=True),
                waits=awts)
            opx("tensor", lambda e, t=t: e.matmul(
                ps_h1b[:, 0:512], h_bf[:, t * 128:(t + 1) * 128],
                woutT_bf[:, 512:1024], start=True, stop=True), inc="t")
            tAT = C["t"]
            wts = [("t", t_r5)] if (c == 0 and t == 0) else []
            opx("vector", lambda e, t=t, p4=p4, xt=xt: e.tensor_scalar(
                s_sb[:], xt[:, t, :], mv_all[:, p4, t, 0:1], r1_all[:, p4, t, :],
                ALU.subtract, ALU.mult), waits=wts, inc="v")
            em.selfsync("vector", "v")
            opx("vector", lambda e: e.tensor_add(
                s_sb[:, 0:512], ps_h1a[:, 0:512], s_sb[:, 0:512]),
                waits=[("t", tAT)])
            opx("vector", lambda e: e.tensor_add(
                s_sb[:, 512:1024], ps_h1b[:, 0:512], s_sb[:, 512:1024]), inc="v")
            vS = C["v"]
            if t == 0 and do_ln:
                ln_vW = emit_ln_ph2(c + 2, ln_sA)
                ln_sB = emit_ln_sqrt2(c + 2, ln_vW)
                emit_ln_ph3(c + 2, ln_sB)
            for k in range(KH):
                opx("tensor", lambda e, k=k, t=t: e.matmul(
                    ps_ga[:, 0:512], h1_bf[:, k, t * 128:(t + 1) * 128],
                    w2T[:, k, 0:512], start=(k == 0), stop=(k == KH - 1)),
                    waits=[("v", PP["g_free"]), ("s", gelu_done[KH - 1])]
                    if k == 0 else ())
            for k in range(KH):
                opx("tensor", lambda e, k=k, t=t: e.matmul(
                    ps_gb[:, 0:512], h1_bf[:, k, t * 128:(t + 1) * 128],
                    w2T[:, k, 512:1024], start=(k == 0), stop=(k == KH - 1)),
                    inc="t" if k == KH - 1 else None)
            tG = C["t"]
            opx("vector", lambda e: e.tensor_add(
                g_sb[:, 0:512], ps_ga[:, 0:512], b2_bc[:, 0:512]),
                waits=[("t", tG)])
            opx("vector", lambda e: e.tensor_add(
                g_sb[:, 512:1024], ps_gb[:, 0:512], b2_bc[:, 512:1024]), inc="v")
            vG = C["v"]
            PP["g_free"] = vG
            em.selfsync("vector", "v")
            opx("vector", lambda e: e.tensor_mul(s_sb[:], s_sb[:], g_sb[:]),
                inc="v")
            vM1 = C["v"]
            opx("scalar", lambda e: e.activation(g_sb[:], g_sb[:], AF.Sigmoid),
                waits=[("v", vM1)], inc="s")
            sSIL = C["s"]
            opx("vector", lambda e: e.tensor_mul(s_sb[:], s_sb[:], g_sb[:]),
                waits=[("s", sSIL)], inc="v")
            em.selfsync("vector", "v")
            opx("vector", lambda e, t=t, xt=xt: e.tensor_add(
                xt[:, t, :], xt[:, t, :], s_sb[:]), inc="v")
            em.selfsync("vector", "v")
        vOUT = C["v"]
        opx("gpsimd", lambda e, c=c, xt=xt: e.dma_start(
            out_d[c * T:(c + 1) * T, :].rearrange("(t p) d -> p t d", p=128),
            xt[:]),
            waits=[("v", vOUT)], inc=("st", "su", "sv")[c % 3], inc_n=16)
        PP["tZ"] = tZ
        PP["sZ"] = sZ
        PP["vB"] = vB
        PP["vScan"] = vScan
        PP["vS_last"] = vS

    for snm in ("st", "su", "sv"):
        fv = C[snm]
        em.ops["sync"].append(lambda e, snm=snm, fv=fv: e.wait_ge(em.sems[snm], fv))

    with nc.Block() as block:

        @block.sync
        def _(e):
            for f in em.ops["sync"]:
                f(e)

        @block.gpsimd
        def _(e):
            for f in em.ops["gpsimd"]:
                f(e)

        @block.vector
        def _(e):
            for f in em.ops["vector"]:
                f(e)

        @block.scalar
        def _(e):
            for f in em.ops["scalar"]:
                f(e)

        @block.tensor
        def _(e):
            for f in em.ops["tensor"]:
                f(e)

    return nc


def kernel(L0_attn_x, L0_attn_W_a, L0_attn_b_a, L0_attn_W_in, L0_attn_b_in,
           L0_attn_W_out, L0_attn_b_out, L0_attn_ln_gamma, L0_attn_ln_beta,
           L0_ffn_w1, L0_ffn_b1, L0_ffn_w2, L0_ffn_b2):
    f = lambda a: np.ascontiguousarray(np.asarray(a, dtype=np.float32))
    x = f(L0_attn_x)
    shared = {
        "wa": f(L0_attn_W_a),
        "wi": f(L0_attn_W_in),
        "wo": f(L0_attn_W_out),
        "ba": f(L0_attn_b_a).reshape(1, HN),
        "bi": f(L0_attn_b_in).reshape(1, HN),
        "bo": f(L0_attn_b_out).reshape(1, D),
        "gm": f(L0_attn_ln_gamma).reshape(1, D),
        "bt": f(L0_attn_ln_beta).reshape(1, D),
        "w1": f(L0_ffn_w1),
        "b1v": f(L0_ffn_b1).reshape(KH, 128),
        "w2": f(L0_ffn_w2),
        "b2v": f(L0_ffn_b2).reshape(1, D),
    }
    nc = build()
    in_maps = [
        {k: np.copy(v) for k, v in dict(shared, x=x[b]).items()} for b in range(8)
    ]
    res = run_bass_kernel_spmd(nc, in_maps, core_ids=list(range(8)))
    return np.stack([res.results[b]["out"] for b in range(8)], axis=0)


# version 16
# speedup vs baseline: 1.1146x; 1.1146x over previous
import numpy as np
import concourse.bass as bass
import concourse.mybir as mybir
from concourse.bass_utils import run_bass_kernel_spmd

F32 = mybir.dt.float32
BF16 = mybir.dt.bfloat16
AF = mybir.ActivationFunctionType
ALU = mybir.AluOpType
AX = mybir.AxisListType

D = 1024
S = 4096
HN = 32
FF = 4096
T = 256
NT = T // 128
NCH = S // T
KD = D // 128
KH = FF // 128
EPS = 1e-5


class Emitter:
    def __init__(self, nc):
        self.nc = nc
        self.ops = {k: [] for k in ("sync", "gpsimd", "vector", "scalar", "tensor")}
        self.cnt = {k: 0 for k in ("ld", "wl", "wd", "we", "w2", "w3", "w4", "w5", "w6", "w7", "le", "lf", "st", "su", "sv", "v", "s", "t", "g")}
        self.sems = {k: nc.alloc_semaphore("sem_" + k) for k in self.cnt}

    def op(self, eng, fn, inc=None, inc_n=1, waits=()):
        for sname, val in waits:
            assert val <= self.cnt[sname], (eng, sname, val, self.cnt[sname])
        if inc is not None:
            self.cnt[inc] += inc_n
        sems = self.sems
        isem = sems[inc] if inc is not None else None

        def run(e, fn=fn, waits=tuple(waits), isem=isem, n=inc_n):
            for sname, val in waits:
                e.wait_ge(sems[sname], val)
            ins = fn(e)
            if isem is not None:
                ins.then_inc(isem, n)

        self.ops[eng].append(run)
        return self.cnt[inc] if inc is not None else None

    def selfsync(self, eng, sem):
        val = self.cnt[sem]
        sems = self.sems

        def run(e, val=val, sem=sem):
            e.wait_ge(sems[sem], val)

        self.ops[eng].append(run)


def build():
    nc = bass.Bass("TRN2", target_bir_lowering=False)

    x_d = nc.dram_tensor("x", [S, D], F32, kind="ExternalInput")
    wa_d = nc.dram_tensor("wa", [HN, D], F32, kind="ExternalInput")
    wi_d = nc.dram_tensor("wi", [HN, D], F32, kind="ExternalInput")
    wo_d = nc.dram_tensor("wo", [D, HN], F32, kind="ExternalInput")
    ba_d = nc.dram_tensor("ba", [1, HN], F32, kind="ExternalInput")
    bi_d = nc.dram_tensor("bi", [1, HN], F32, kind="ExternalInput")
    bo_d = nc.dram_tensor("bo", [1, D], F32, kind="ExternalInput")
    gm_d = nc.dram_tensor("gm", [1, D], F32, kind="ExternalInput")
    bt_d = nc.dram_tensor("bt", [1, D], F32, kind="ExternalInput")
    w1_d = nc.dram_tensor("w1", [FF, D], F32, kind="ExternalInput")
    b1_d = nc.dram_tensor("b1v", [KH, 128], F32, kind="ExternalInput")
    w2_d = nc.dram_tensor("w2", [D, FF], F32, kind="ExternalInput")
    b2_d = nc.dram_tensor("b2v", [1, D], F32, kind="ExternalInput")
    out_d = nc.dram_tensor("out", [S, D], F32, kind="ExternalOutput")

    w1T = nc.alloc_sbuf_tensor("w1T", [128, KD, KH, 128], BF16)
    w2T = nc.alloc_sbuf_tensor("w2T", [128, KH, 1024], BF16)
    waT_bf = nc.alloc_sbuf_tensor("waT_bf", [128, KD, HN], BF16)
    winT_bf = nc.alloc_sbuf_tensor("winT_bf", [128, KD, HN], BF16)
    woutT_bf = nc.alloc_sbuf_tensor("woutT_bf", [HN + 1, D], BF16)
    gamma_f = nc.alloc_sbuf_tensor("gamma_f", [128, KD], F32)
    beta_f = nc.alloc_sbuf_tensor("beta_f", [128, KD], F32)
    b1_col = nc.alloc_sbuf_tensor("b1_col", [128, KH], F32)
    ba_col = nc.alloc_sbuf_tensor("ba_col", [HN, 2], F32)
    bab_p = nc.alloc_sbuf_tensor("bab_p", [HN, 2], F32)
    b2_bc = nc.alloc_sbuf_tensor("b2_bc", [128, D], F32)
    ident_b = nc.alloc_sbuf_tensor("ident_b", [128, 128], BF16)
    ones_f = nc.alloc_sbuf_tensor("ones_f", [1, 128], F32)

    x_sbA = nc.alloc_sbuf_tensor("x_sbA", [128, NT, D], F32)
    x_sbB = nc.alloc_sbuf_tensor("x_sbB", [128, NT, D], F32)
    x_sbC = nc.alloc_sbuf_tensor("x_sbC", [128, NT, D], F32)
    xnf_bfA = nc.alloc_sbuf_tensor("xnf_bfA", [128, KD, T], BF16)
    xnf_bfB = nc.alloc_sbuf_tensor("xnf_bfB", [128, KD, T], BF16)
    uf_bf = nc.alloc_sbuf_tensor("uf_bf", [128, KD, T], BF16)
    h1_bf = nc.alloc_sbuf_tensor("h1_bf", [128, KH, T], BF16)
    g_sb = nc.alloc_sbuf_tensor("g_sb", [128, D], F32)
    s_sb = nc.alloc_sbuf_tensor("s_sb", [128, D], F32)
    stage = nc.alloc_sbuf_tensor("stage", [128, 4 * D], BF16)
    a_s = nc.alloc_sbuf_tensor("a_s", [HN, T], F32)
    b_s = nc.alloc_sbuf_tensor("b_s", [HN, T], F32)
    h_s = nc.alloc_sbuf_tensor("h_s", [HN, T], F32)
    h_bf = nc.alloc_sbuf_tensor("h_bf", [HN + 1, T], BF16)
    carry = nc.alloc_sbuf_tensor("carry", [HN, 1], F32)
    st6 = nc.alloc_sbuf_tensor("st6", [128, 12], F32)
    mv_all = nc.alloc_sbuf_tensor("mv_all", [128, 4, NT, 2], F32)
    r1_all = nc.alloc_sbuf_tensor("r1_all", [128, 4, NT, 1], F32)
    r12_all = nc.alloc_sbuf_tensor("r12_all", [128, NT], F32)
    veps = nc.alloc_sbuf_tensor("veps", [128, NT], F32)
    stdt = nc.alloc_sbuf_tensor("stdt", [128, 2 * NT], F32)
    w2t = nc.alloc_sbuf_tensor("w2t", [128, NT], F32)
    pw = veps
    alph = nc.alloc_sbuf_tensor("alph", [128, 8], F32)

    h1_flat = h1_bf[:].rearrange("p a b -> p (a b)")
    wstg = lambda a, b: h1_flat[:, a:b]
    x_flat = x_sbA[:].rearrange("p a b -> p (a b)")
    wa_st = lambda lo, j0, j1: x_flat[lo:lo + HN, j0:j1]
    waT_f = s_sb[:][:, 0:256].rearrange("p (a b) -> p a b", b=HN)
    winT_f = s_sb[:][:, 256:512].rearrange("p (a b) -> p a b", b=HN)
    wo_sb = g_sb[:][:, 0:256].rearrange("p (a b) -> p a b", b=HN)
    xB_flat = x_sbB[:].rearrange("p a b -> p (a b)")
    rows_sb = xB_flat[0:1, 0:D]
    xC_flat = x_sbC[:].rearrange("p a b -> p (a b)")
    ident_f = xC_flat[:, 0:128]
    b1v_sb = xC_flat[0:HN, 128:256]
    bab_sb = xC_flat[0:1, 256:256 + 2 * HN]

    ps_tr0 = nc.alloc_psum_tensor("ps_tr0", [128, 1024], BF16)
    ps_tr1 = nc.alloc_psum_tensor("ps_tr1", [128, 1024], BF16)
    ps_zab = nc.alloc_psum_tensor("ps_zab", [HN, 512], F32)
    ps_h1a = nc.alloc_psum_tensor("ps_h1a", [128, 512], F32)
    ps_h1b = nc.alloc_psum_tensor("ps_h1b", [128, 512], F32)
    ps_ga = nc.alloc_psum_tensor("ps_ga", [128, 512], F32)
    ps_gb = nc.alloc_psum_tensor("ps_gb", [128, 512], F32)
    ps_ms = nc.alloc_psum_tensor("ps_ms", [128, 512], F32)

    em = Emitter(nc)
    opx = em.op
    C = em.cnt

    opx("gpsimd", lambda e: e.memset(ident_f, 0.0), inc="g")
    g_if0 = C["g"]
    opx("gpsimd", lambda e: e.affine_select(
        out=ident_f, in_=ident_f, compare_op=ALU.not_equal, fill=1.0,
        base=0, pattern=[[-1, 128]], channel_multiplier=1),
        inc="g", waits=[("g", g_if0)])
    opx("gpsimd", lambda e: e.memset(ident_b[:], 0.0), inc="g")
    g_ib0 = C["g"]
    opx("gpsimd", lambda e: e.affine_select(
        out=ident_b[:], in_=ident_b[:], compare_op=ALU.not_equal, fill=1.0,
        base=0, pattern=[[-1, 128]], channel_multiplier=1),
        inc="g", waits=[("g", g_ib0)])
    opx("gpsimd", lambda e: e.memset(ones_f[:], 1.0), inc="g")
    opx("gpsimd", lambda e: e.memset(h_bf[HN:HN + 1, :], 1.0), inc="g")
    g_setup = C["g"]

    opx("gpsimd", lambda e: e.dma_start(wa_st(0, 0, D), wa_d[:]), inc="wl", inc_n=16)
    opx("gpsimd", lambda e: e.dma_start(
        wo_sb, wo_d[:].rearrange("(a p) c -> p a c", p=128)), inc="wl", inc_n=16)
    opx("gpsimd", lambda e: e.dma_start(bab_sb[0:1, 0:HN], ba_d[:]), inc="wl", inc_n=16)
    opx("gpsimd", lambda e: e.dma_start(bab_sb[0:1, HN:2 * HN], bi_d[:]), inc="wl", inc_n=16)
    opx("gpsimd", lambda e: e.dma_start(rows_sb, gm_d[:]), inc="wl", inc_n=16)
    opx("gpsimd", lambda e: e.dma_start(b1v_sb, b1_d[:]), inc="wl", inc_n=16)
    opx("gpsimd", lambda e: e.dma_start(woutT_bf[HN:HN + 1, :], bo_d[:]), inc="wl", inc_n=16)
    wl_small = C["wl"]

    opx("vector", lambda e: e.memset(carry[:], 0.0), inc="v")

    for j in range(KD):
        opx("tensor", lambda e, j=j: e.transpose(
            ps_ms[:, j * HN:(j + 1) * HN], wa_st(0, j * 128, (j + 1) * 128),
            ident_f[0:HN, 0:HN]),
            waits=[("wl", wl_small), ("g", g_setup)] if j == 0 else (),
            inc="t" if j == KD - 1 else None)
    t_r1a = C["t"]
    opx("vector", lambda e: e.tensor_copy(
        waT_f, ps_ms[:, 0:256].rearrange("p (a b) -> p a b", b=HN)),
        waits=[("t", t_r1a)], inc="v")
    v_r1a = C["v"]
    opx("gpsimd", lambda e: e.dma_start(wa_st(0, 0, D), wi_d[:]),
        inc="wl", inc_n=16, waits=[("t", t_r1a)])
    wl_wi = C["wl"]
    for j in range(KD):
        opx("tensor", lambda e, j=j: e.transpose(
            ps_ms[:, 256 + j * HN:256 + (j + 1) * HN],
            wa_st(0, j * 128, (j + 1) * 128), ident_f[0:HN, 0:HN]),
            waits=[("wl", wl_wi)] if j == 0 else (),
            inc="t" if j == KD - 1 else None)
    t_r1 = C["t"]
    opx("vector", lambda e: e.tensor_copy(
        winT_f, ps_ms[:, 256:512].rearrange("p (a b) -> p a b", b=HN)),
        waits=[("t", t_r1)], inc="v")
    v_r1 = C["v"]

    for j in range(KD):
        opx("tensor", lambda e, j=j: e.transpose(
            ps_ms[:, j:j + 1], rows_sb[:, j * 128:(j + 1) * 128],
            ident_f[0:1, 0:1]),
            waits=[("v", v_r1)] if j == 0 else ())
    opx("tensor", lambda e: e.transpose(
        ps_ms[:, 16:48], b1v_sb, ident_f[0:HN, 0:HN]))
    opx("tensor", lambda e: e.transpose(
        ps_ms[0:HN, 48:49], bab_sb[0:1, 0:HN], ident_f[0:1, 0:1]))
    opx("tensor", lambda e: e.transpose(
        ps_ms[0:HN, 49:50], bab_sb[0:1, HN:2 * HN], ident_f[0:1, 0:1]), inc="t")
    t_r2a = C["t"]
    opx("vector", lambda e: e.tensor_copy(gamma_f[:], ps_ms[:, 0:8]),
        waits=[("t", t_r2a)])
    opx("vector", lambda e: e.tensor_copy(b1_col[:], ps_ms[:, 16:48]))
    opx("vector", lambda e: e.tensor_copy(ba_col[:], ps_ms[0:HN, 48:50]), inc="v")
    v_r2a = C["v"]
    em.selfsync("vector", "v")
    opx("gpsimd", lambda e: e.dma_start(rows_sb, bt_d[:]),
        inc="wl", inc_n=16, waits=[("t", t_r2a)])
    wl_bt = C["wl"]
    for j in range(KD):
        opx("tensor", lambda e, j=j: e.transpose(
            ps_ms[:, 8 + j:9 + j], rows_sb[:, j * 128:(j + 1) * 128],
            ident_f[0:1, 0:1]),
            waits=[("wl", wl_bt), ("v", v_r2a)] if j == 0 else (),
            inc="t" if j == KD - 1 else None)
    t_r2b = C["t"]
    opx("vector", lambda e: e.tensor_copy(beta_f[:], ps_ms[:, 8:16]),
        waits=[("t", t_r2b)], inc="v")
    v_r2 = C["v"]
    em.selfsync("vector", "v")
    opx("gpsimd", lambda e: e.dma_start(rows_sb, b2_d[:]),
        inc="wl", inc_n=16, waits=[("t", t_r2b)])
    wl_b2 = C["wl"]

    v_evac_r3 = v_r2
    for h in range(2):
        for j in range(4):
            a = h * 4 + j
            opx("tensor", lambda e, a=a, j=j: e.transpose(
                ps_ms[0:HN, j * 128:(j + 1) * 128], wo_sb[:, a, :], ident_f),
                waits=[("v", v_evac_r3)] if j == 0 else (),
                inc="t" if j == 3 else None)
        t_r3 = C["t"]
        opx("vector", lambda e, h=h: e.tensor_copy(
            woutT_bf[0:HN, h * 512:(h + 1) * 512], ps_ms[0:HN, 0:512]),
            waits=[("t", t_r3)], inc="v")
        v_evac_r3 = C["v"]
    v_r3 = C["v"]

    for j in range(KD):
        opx("vector", lambda e, j=j: e.tensor_scalar_mul(
            waT_bf[:, j, :], waT_f[:, j, :], gamma_f[:, j:j + 1]))
    for j in range(KD):
        opx("vector", lambda e, j=j: e.tensor_scalar_mul(
            winT_bf[:, j, :], winT_f[:, j, :], gamma_f[:, j:j + 1]),
            inc="v" if j == KD - 1 else None)
    v_r4 = C["v"]

    for j in range(KD):
        opx("tensor", lambda e, j=j: e.matmul(
            ps_ms[0:HN, 0:1], waT_f[:, j, :], beta_f[:, j:j + 1],
            start=(j == 0), stop=(j == KD - 1)),
            waits=[("v", v_r3)] if j == 0 else ())
    for j in range(KD):
        opx("tensor", lambda e, j=j: e.matmul(
            ps_ms[0:HN, 1:2], winT_f[:, j, :], beta_f[:, j:j + 1],
            start=(j == 0), stop=(j == KD - 1)),
            inc="t" if j == KD - 1 else None)
    t_r5 = C["t"]
    opx("vector", lambda e: e.tensor_add(bab_p[:], ps_ms[0:HN, 0:2], ba_col[:]),
        waits=[("t", t_r5)], inc="v")
    v_r5 = C["v"]

    opx("tensor", lambda e: e.matmul(
        ps_ms[:, 0:512], ones_f[0:1, :], rows_sb[:, 0:512],
        start=True, stop=True),
        waits=[("v", v_r5), ("wl", wl_b2)], inc="t")
    t_b2a = C["t"]
    opx("vector", lambda e: e.tensor_copy(b2_bc[:, 0:512], ps_ms[:, 0:512]),
        waits=[("t", t_b2a)], inc="v")
    v_b2a = C["v"]
    opx("tensor", lambda e: e.matmul(
        ps_ms[:, 0:512], ones_f[0:1, :], rows_sb[:, 512:1024],
        start=True, stop=True),
        waits=[("v", v_b2a)], inc="t")
    t_b2b = C["t"]
    opx("vector", lambda e: e.tensor_copy(b2_bc[:, 512:1024], ps_ms[:, 0:512]),
        waits=[("t", t_b2b)], inc="v")
    v_b2_done = C["v"]


    ps_tr = [ps_tr0, ps_tr1]
    tr_free = {0: ("v", 0), 1: ("v", 0)}
    tr_fill = {0: 0, 1: 0}

    def trans8(bank, src_slices, dst_ap_fn, extra_waits, evac_eng="vector"):
        ps = ps_tr[bank]
        wts = list(extra_waits) + [tr_free[bank]]
        for k, src in enumerate(src_slices):
            opx("tensor", lambda e, k=k, src=src, ps=ps: e.transpose(
                ps[:, k * 128:(k + 1) * 128], src, ident_b[:]),
                waits=wts if k == 0 else (), inc="t" if k == len(src_slices) - 1 else None)
        tr_fill[bank] = C["t"]
        if evac_eng == "vector":
            opx("vector", lambda e, ps=ps: e.tensor_copy(
                dst_ap_fn(), ps[:].rearrange("p (a b) -> p a b", b=128)),
                waits=[("t", tr_fill[bank])], inc="v")
            tr_free[bank] = ("v", C["v"])
        else:
            opx("scalar", lambda e, ps=ps: e.copy(
                dst_ap_fn(), ps[:].rearrange("p (a b) -> p a b", b=128)),
                waits=[("t", tr_fill[bank])], inc="s")
            tr_free[bank] = ("s", C["s"])
        return tr_fill[bank]

    w1_fill = [0] * KH
    w2_fill = [0] * KD
    wsems = ["wd", "we", "w2", "w3", "w4", "w5", "w6", "w7"]

    def w1_part(i):
        o = (i % 4) * 1024
        a = (i % 3) * 2
        psem = wsems[i % 4]
        wl_i = C[psem]
        opx("vector", lambda e, o=o, a=a: e.reduce_sum(
            alph[:, a:a + 1], wstg(o, o + 1024), AX.X, apply_absolute_value=True),
            waits=[(psem, wl_i)], inc="v")
        em.selfsync("vector", "v")
        opx("vector", lambda e, a=a: e.tensor_scalar_mul(
            alph[:, a + 1:a + 2], alph[:, a:a + 1], 1.0 / D), inc="v")
        v_alpha = C["v"]
        em.selfsync("vector", "v")
        opx("scalar", lambda e, o=o: e.sign(wstg(o, o + 1024), wstg(o, o + 1024)),
            waits=[("v", v_alpha)], inc="s")
        return C["s"]

    def w1_tail(i, s_sign):
        o = (i % 4) * 1024
        a = (i % 3) * 2
        opx("vector", lambda e, o=o, a=a: e.tensor_scalar_mul(
            wstg(o, o + 1024), wstg(o, o + 1024), alph[:, a + 1:a + 2]),
            waits=[("s", s_sign)], inc="v")
        v_fold = C["v"]
        srcs = [wstg(o + k * 128, o + (k + 1) * 128) for k in range(KD)]
        w1_fill[i] = trans8(
            i % 2, srcs, (lambda i=i: w1T[:, :, i, :]), [("v", v_fold)],
            evac_eng="scalar" if i % 2 == 1 else "vector")

    def w2_round(j, r):
        o = 4096
        srcs = [wstg(o + (r * 8 + jj) * 128, o + (r * 8 + jj + 1) * 128)
                for jj in range(8)]
        w2_fill[j] = trans8(
            r % 2, srcs,
            (lambda j=j, r=r: w2T[:, r * 8:(r + 1) * 8, j * 128:(j + 1) * 128]),
            [("v", w2_fold[j])] if r == 0 else [],
            evac_eng="scalar")

    w2_fold = {}

    def w2_tile(j):
        o = 4096
        psem = wsems[4]
        if j >= 1:
            for r in range(4):
                w2_round(j - 1, r)
        wts = [("t", w2_fill[j - 1])] if j >= 1 else []
        opx("gpsimd", lambda e, j=j, o=o: e.dma_start(
            wstg(o, o + 4096), w2_d[j * 128:(j + 1) * 128, :]),
            inc=psem, inc_n=16, waits=wts)
        wl_j = C[psem]
        for r in range(4):
            opx("vector", lambda e, o=o, r=r: e.reduce_sum(
                stdt[:, r:r + 1], wstg(o + r * 1024, o + (r + 1) * 1024),
                AX.X, apply_absolute_value=True),
                waits=[(psem, wl_j)] if r == 0 else (),
                inc="v" if r == 3 else None)
        em.selfsync("vector", "v")
        opx("vector", lambda e: e.tensor_add(
            alph[:, 6:7], stdt[:, 0:1], stdt[:, 1:2]))
        opx("vector", lambda e: e.tensor_add(
            stdt[:, 2:3], stdt[:, 2:3], stdt[:, 3:4]), inc="v")
        em.selfsync("vector", "v")
        opx("vector", lambda e: e.tensor_add(
            alph[:, 6:7], alph[:, 6:7], stdt[:, 2:3]), inc="v")
        em.selfsync("vector", "v")
        opx("vector", lambda e: e.tensor_scalar_mul(
            alph[:, 7:8], alph[:, 6:7], 1.0 / FF), inc="v")
        v_alpha = C["v"]
        em.selfsync("vector", "v")
        opx("scalar", lambda e, o=o: e.sign(wstg(o, o + 4096), wstg(o, o + 4096)),
            waits=[("v", v_alpha)], inc="s")
        s_sign = C["s"]
        opx("vector", lambda e, o=o: e.tensor_scalar_mul(
            wstg(o, o + 4096), wstg(o, o + 4096), alph[:, 7:8]),
            waits=[("s", s_sign)], inc="v")
        w2_fold[j] = C["v"]

    w1_sign = {}
    for i in range(KH):
        if i % 4 == 0:
            w2_tile(i // 4)
        wts = [("t", w1_fill[i - 4])] if i >= 4 else []
        psem = wsems[i % 4]
        opx("gpsimd", lambda e, i=i: e.dma_start(
            wstg((i % 4) * 1024, (i % 4) * 1024 + 1024),
            w1_d[i * 128:(i + 1) * 128, :]),
            inc=psem, inc_n=16, waits=wts)
        w1_sign[i] = w1_part(i)
        if i >= 1:
            w1_tail(i - 1, w1_sign[i - 1])
    w1_tail(KH - 1, w1_sign[KH - 1])
    for r in range(4):
        w2_round(KD - 1, r)

    prologue_end = dict(C)

    x_par = [x_sbA, x_sbB, x_sbC]
    ldsem = ["ld", "le", "lf"]
    xload_cnt = {}
    stage_free = {j: 0 for j in range(4)}
    PP = dict(vU=[0, 0], vXN=[0, 0], tZ=0, sZ=0, vB=0, vScan=0,
              gelu_tail=[0, 0, 0], g_free=0, lnS=0, uf_ev=("v", 0),
              xnf_ev=("v", 0), vS_last=0)
    xnf_par = [xnf_bfA, xnf_bfB]

    def emit_trans_group(cc, slot):
        t = slot // 2
        is_xn = slot % 2
        pcc = cc % 2
        if is_xn:
            stage_free[slot] = trans8(
                1, [stage[:, slot * D + k * 128:slot * D + (k + 1) * 128]
                    for k in range(KD)],
                (lambda t=t, pcc=pcc: xnf_par[pcc][:, :, t * 128:(t + 1) * 128]),
                [("v", PP["vXN"][t])])
            PP["xnf_ev"] = tr_free[1]
        else:
            stage_free[slot] = trans8(
                0, [stage[:, slot * D + k * 128:slot * D + (k + 1) * 128]
                    for k in range(KD)],
                (lambda t=t: uf_bf[:, :, t * 128:(t + 1) * 128]),
                [("v", PP["vU"][t])])
            PP["uf_ev"] = tr_free[0]
    gelu_done = [0] * KH

    def emit_xload(c):
        p = c % 3
        if c == 0:
            wts = [("t", t_r1)]
        elif c == 1:
            wts = [("t", t_b2b)]
        elif c == 2:
            wts = [("t", t_r3)]
        else:
            wts = [(("st", "su", "sv")[c % 3], 16 * ((c - 3) // 3 + 1))]
        opx("sync", lambda e, c=c, p=p: e.dma_start(
            x_par[p][:], x_d[c * T:(c + 1) * T, :].rearrange(
                "(t p) d -> p t d", p=128)),
            inc=ldsem[p], inc_n=16, waits=wts)
        xload_cnt[c] = C[ldsem[p]]

    def emit_ln_ph1(cc):
        p = cc % 4
        xt = x_par[cc % 3]
        vA = [0, 0]
        for t in range(NT):
            wts = []
            if t == 0:
                wts = [(ldsem[cc % 3], xload_cnt[cc]), ("s", PP["lnS"])]
            opx("vector", lambda e, t=t, xt=xt: e.bn_stats(
                st6[:, 0:6], xt[:, t, 0:512]), waits=wts)
            opx("vector", lambda e, t=t, xt=xt: e.bn_stats(
                st6[:, 6:12], xt[:, t, 512:1024]), inc="v")
            em.selfsync("vector", "v")
            opx("vector", lambda e, t=t, p=p: e.bn_aggr(
                mv_all[:, p, t, :], st6[:]), inc="v")
            em.selfsync("vector", "v")
            opx("vector", lambda e, t=t, p=p: e.tensor_scalar_add(
                veps[:, t:t + 1], mv_all[:, p, t, 1:2], EPS), inc="v")
            vA[t] = C["v"]
        return vA

    def emit_ln_sqrt1(cc, vA):
        sA = [0, 0]
        for t in range(NT):
            opx("scalar", lambda e, t=t: e.sqrt(
                stdt[:, t:t + 1], veps[:, t:t + 1]),
                waits=[("v", vA[t])], inc="s")
            sA[t] = C["s"]
        return sA

    def emit_ln_ph2(cc, sA):
        p = cc % 4
        vW = [0, 0]
        for t in range(NT):
            opx("vector", lambda e, t=t, p=p: e.reciprocal(
                r1_all[:, p, t, :], stdt[:, t:t + 1]),
                waits=[("s", sA[t])], inc="v")
            em.selfsync("vector", "v")
            opx("vector", lambda e, t=t, p=p: e.tensor_mul(
                pw[:, t:t + 1], r1_all[:, p, t, :], r1_all[:, p, t, :]), inc="v")
            em.selfsync("vector", "v")
            opx("vector", lambda e, t=t, p=p: e.tensor_scalar(
                w2t[:, t:t + 1], pw[:, t:t + 1], mv_all[:, p, t, 1:2], EPS,
                ALU.mult, ALU.add), inc="v")
            vW[t] = C["v"]
        return vW

    def emit_ln_sqrt2(cc, vW):
        sB = [0, 0]
        for t in range(NT):
            opx("scalar", lambda e, t=t: e.sqrt(
                stdt[:, NT + t:NT + t + 1], w2t[:, t:t + 1]),
                waits=[("v", vW[t])], inc="s")
            sB[t] = C["s"]
        PP["lnS"] = C["s"]
        return sB

    def emit_ln_ph3(cc, sB):
        p = cc % 4
        xt = x_par[cc % 3]
        for t in range(NT):
            opx("vector", lambda e, t=t: e.reciprocal(
                pw[:, t:t + 1], stdt[:, NT + t:NT + t + 1]),
                waits=[("s", sB[t])], inc="v")
            em.selfsync("vector", "v")
            opx("vector", lambda e, t=t, p=p: e.tensor_mul(
                r12_all[:, t:t + 1], r1_all[:, p, t, :], pw[:, t:t + 1]), inc="v")
            em.selfsync("vector", "v")
            opx("vector", lambda e, t=t, p=p, xt=xt: e.tensor_scalar(
                stage[:, (2 * t) * D:(2 * t + 1) * D], xt[:, t, :],
                mv_all[:, p, t, 0:1], r12_all[:, t:t + 1],
                ALU.subtract, ALU.mult),
                waits=[("t", stage_free[2 * t])], inc="v")
            PP["vU"][t] = C["v"]
            opx("vector", lambda e, t=t, p=p, xt=xt: e.tensor_scalar(
                stage[:, (2 * t + 1) * D:(2 * t + 2) * D], xt[:, t, :],
                mv_all[:, p, t, 0:1], r1_all[:, p, t, :],
                ALU.subtract, ALU.mult),
                waits=[("t", stage_free[2 * t + 1])], inc="v")
            PP["vXN"][t] = C["v"]

    def emit_ln_block(cc):
        vA = emit_ln_ph1(cc)
        sA = emit_ln_sqrt1(cc, vA)
        vW = emit_ln_ph2(cc, sA)
        sB = emit_ln_sqrt2(cc, vW)
        emit_ln_ph3(cc, sB)

    emit_xload(0)
    emit_xload(1)
    emit_ln_block(0)
    for slot in range(4):
        emit_trans_group(0, slot)
    emit_ln_block(1)

    for c in range(NCH):
        p = c % 3
        xt = x_par[p]
        pf = c % 2
        p4 = c % 4
        if c + 2 < NCH:
            emit_xload(c + 2)

        uf_ev = PP["uf_ev"]
        xnf_ev = PP["xnf_ev"]
        xnf_c = xnf_par[pf]

        for k in range(KD):
            opx("tensor", lambda e, k=k: e.matmul(
                ps_zab[:, 0:T], waT_bf[:, k, :], uf_bf[:, k, :],
                start=(k == 0), stop=(k == KD - 1)),
                waits=[uf_ev, ("s", PP["sZ"]), ("v", PP["vB"])]
                if k == 0 else ())
        for k in range(KD):
            opx("tensor", lambda e, k=k: e.matmul(
                ps_zab[:, 256:256 + T], winT_bf[:, k, :], uf_bf[:, k, :],
                start=(k == 0), stop=(k == KD - 1)),
                inc="t" if k == KD - 1 else None)
        tZ = C["t"]

        opx("scalar", lambda e: e.activation(
            a_s[:], ps_zab[:, 0:T], AF.Sigmoid, bias=bab_p[:, 0:1]),
            waits=[("t", tZ), ("v", PP["vScan"])], inc="s")
        sZ = C["s"]
        opx("scalar", lambda e: e.activation(alph[:, 0:1], alph[:, 0:1], AF.Gelu))
        opx("vector", lambda e: e.tensor_scalar_add(
            b_s[:], ps_zab[:, 256:256 + T], bab_p[:, 1:2]), waits=[("t", tZ)],
            inc="v")
        vB = C["v"]
        em.selfsync("vector", "v")
        opx("vector", lambda e: e.tensor_tensor_scan(
            h_s[:], a_s[:], b_s[:], carry[:, 0:1], ALU.mult, ALU.add),
            waits=[("s", sZ)], inc="v")
        vScan = C["v"]
        em.selfsync("vector", "v")
        opx("vector", lambda e: e.tensor_copy(carry[:], h_s[:, T - 1:T]), inc="v")
        em.selfsync("vector", "v")
        opx("vector", lambda e: e.tensor_copy(h_bf[0:HN, :], h_s[:]), inc="v")
        vH = C["v"]

        h1_banks = [ps_h1a, ps_h1b, ps_ms]
        for m in range(KH):
            if c + 1 < NCH and m in (13, 16, 19, 22):
                emit_trans_group(c + 1, (m - 13) // 3)
            ps = h1_banks[m % 3]
            if m == 0:
                wts = [xnf_ev, ("s", PP["gelu_tail"][0]),
                       ("v", PP["vS_last"])]
            elif m in (1, 2):
                wts = [("s", PP["gelu_tail"][m])]
                if m == 2 and c == 0:
                    wts.append(("v", v_b2_done))
            else:
                wts = [("s", gelu_done[m - 3])]
            for k in range(KD):
                opx("tensor", lambda e, k=k, m=m, ps=ps, xnf_c=xnf_c: e.matmul(
                    ps[:, 0:T], w1T[:, k, m, :], xnf_c[:, k, :],
                    start=(k == 0), stop=(k == KD - 1)),
                    waits=wts if k == 0 else (),
                    inc="t" if k == KD - 1 else None)
            tM = C["t"]
            opx("scalar", lambda e, m=m, ps=ps: e.activation(
                h1_bf[:, m, :], ps[:, 0:T], AF.Gelu, bias=b1_col[:, m:m + 1]),
                waits=[("t", tM)], inc="s")
            gelu_done[m] = C["s"]
        PP["gelu_tail"] = [gelu_done[30], gelu_done[31], gelu_done[29]]

        do_ln = c + 2 < NCH
        if do_ln:
            ln_vA = emit_ln_ph1(c + 2)
            ln_sA = emit_ln_sqrt1(c + 2, ln_vA)

        vS = 0
        for t in range(NT):
            awts = [("v", vH), ("s", gelu_done[KH - 1])] if t == 0 else [("v", vS)]
            opx("tensor", lambda e, t=t: e.matmul(
                ps_h1a[:, 0:512], h_bf[:, t * 128:(t + 1) * 128],
                woutT_bf[:, 0:512], start=True, stop=True),
                waits=awts)
            opx("tensor", lambda e, t=t: e.matmul(
                ps_h1b[:, 0:512], h_bf[:, t * 128:(t + 1) * 128],
                woutT_bf[:, 512:1024], start=True, stop=True), inc="t")
            tAT = C["t"]
            wts = [("t", t_r5)] if (c == 0 and t == 0) else []
            opx("vector", lambda e, t=t, p4=p4, xt=xt: e.tensor_scalar(
                s_sb[:], xt[:, t, :], mv_all[:, p4, t, 0:1], r1_all[:, p4, t, :],
                ALU.subtract, ALU.mult), waits=wts, inc="v")
            em.selfsync("vector", "v")
            opx("vector", lambda e: e.tensor_add(
                s_sb[:, 0:512], ps_h1a[:, 0:512], s_sb[:, 0:512]),
                waits=[("t", tAT)])
            opx("vector", lambda e: e.tensor_add(
                s_sb[:, 512:1024], ps_h1b[:, 0:512], s_sb[:, 512:1024]), inc="v")
            vS = C["v"]
            if t == 0 and do_ln:
                ln_vW = emit_ln_ph2(c + 2, ln_sA)
                ln_sB = emit_ln_sqrt2(c + 2, ln_vW)
                emit_ln_ph3(c + 2, ln_sB)
            for k in range(KH):
                opx("tensor", lambda e, k=k, t=t: e.matmul(
                    ps_ga[:, 0:512], h1_bf[:, k, t * 128:(t + 1) * 128],
                    w2T[:, k, 0:512], start=(k == 0), stop=(k == KH - 1)),
                    waits=[("v", PP["g_free"]), ("s", gelu_done[KH - 1])]
                    if k == 0 else ())
            for k in range(KH):
                opx("tensor", lambda e, k=k, t=t: e.matmul(
                    ps_gb[:, 0:512], h1_bf[:, k, t * 128:(t + 1) * 128],
                    w2T[:, k, 512:1024], start=(k == 0), stop=(k == KH - 1)),
                    inc="t" if k == KH - 1 else None)
            tG = C["t"]
            opx("vector", lambda e: e.tensor_add(
                g_sb[:, 0:512], ps_ga[:, 0:512], b2_bc[:, 0:512]),
                waits=[("t", tG)])
            opx("vector", lambda e: e.tensor_add(
                g_sb[:, 512:1024], ps_gb[:, 0:512], b2_bc[:, 512:1024]), inc="v")
            vG = C["v"]
            PP["g_free"] = vG
            em.selfsync("vector", "v")
            opx("vector", lambda e: e.tensor_mul(s_sb[:], s_sb[:], g_sb[:]),
                inc="v")
            vM1 = C["v"]
            opx("scalar", lambda e: e.activation(g_sb[:], g_sb[:], AF.Sigmoid),
                waits=[("v", vM1)], inc="s")
            sSIL = C["s"]
            opx("vector", lambda e: e.tensor_mul(s_sb[:], s_sb[:], g_sb[:]),
                waits=[("s", sSIL)], inc="v")
            em.selfsync("vector", "v")
            opx("vector", lambda e, t=t, xt=xt: e.tensor_add(
                xt[:, t, :], xt[:, t, :], s_sb[:]), inc="v")
            em.selfsync("vector", "v")
        vOUT = C["v"]
        opx("gpsimd", lambda e, c=c, xt=xt: e.dma_start(
            out_d[c * T:(c + 1) * T, :].rearrange("(t p) d -> p t d", p=128),
            xt[:]),
            waits=[("v", vOUT)], inc=("st", "su", "sv")[c % 3], inc_n=16)
        PP["tZ"] = tZ
        PP["sZ"] = sZ
        PP["vB"] = vB
        PP["vScan"] = vScan
        PP["vS_last"] = vS

    for snm in ("st", "su", "sv"):
        fv = C[snm]
        em.ops["sync"].append(lambda e, snm=snm, fv=fv: e.wait_ge(em.sems[snm], fv))

    with nc.Block() as block:

        @block.sync
        def _(e):
            for f in em.ops["sync"]:
                f(e)

        @block.gpsimd
        def _(e):
            for f in em.ops["gpsimd"]:
                f(e)

        @block.vector
        def _(e):
            for f in em.ops["vector"]:
                f(e)

        @block.scalar
        def _(e):
            for f in em.ops["scalar"]:
                f(e)

        @block.tensor
        def _(e):
            for f in em.ops["tensor"]:
                f(e)

    return nc


def kernel(L0_attn_x, L0_attn_W_a, L0_attn_b_a, L0_attn_W_in, L0_attn_b_in,
           L0_attn_W_out, L0_attn_b_out, L0_attn_ln_gamma, L0_attn_ln_beta,
           L0_ffn_w1, L0_ffn_b1, L0_ffn_w2, L0_ffn_b2):
    f = lambda a: np.ascontiguousarray(np.asarray(a, dtype=np.float32))
    x = f(L0_attn_x)
    shared = {
        "wa": f(L0_attn_W_a),
        "wi": f(L0_attn_W_in),
        "wo": f(L0_attn_W_out),
        "ba": f(L0_attn_b_a).reshape(1, HN),
        "bi": f(L0_attn_b_in).reshape(1, HN),
        "bo": f(L0_attn_b_out).reshape(1, D),
        "gm": f(L0_attn_ln_gamma).reshape(1, D),
        "bt": f(L0_attn_ln_beta).reshape(1, D),
        "w1": f(L0_ffn_w1),
        "b1v": f(L0_ffn_b1).reshape(KH, 128),
        "w2": f(L0_ffn_w2),
        "b2v": f(L0_ffn_b2).reshape(1, D),
    }
    nc = build()
    in_maps = [
        {k: np.copy(v) for k, v in dict(shared, x=x[b]).items()} for b in range(8)
    ]
    res = run_bass_kernel_spmd(nc, in_maps, core_ids=list(range(8)))
    return np.stack([res.results[b]["out"] for b in range(8)], axis=0)


# version 17
# speedup vs baseline: 1.1933x; 1.0706x over previous
import numpy as np
import concourse.bass as bass
import concourse.mybir as mybir
from concourse.bass_utils import run_bass_kernel_spmd

F32 = mybir.dt.float32
BF16 = mybir.dt.bfloat16
AF = mybir.ActivationFunctionType
ALU = mybir.AluOpType
AX = mybir.AxisListType

D = 1024
S = 4096
HN = 32
FF = 4096
T = 256
NT = T // 128
NCH = S // T
KD = D // 128
KH = FF // 128
EPS = 1e-5


class Emitter:
    def __init__(self, nc):
        self.nc = nc
        self.ops = {k: [] for k in ("sync", "gpsimd", "vector", "scalar", "tensor")}
        self.cnt = {k: 0 for k in ("ld", "wl", "wd", "we", "w2", "w3", "w4", "w5", "w6", "w7", "le", "lf", "st", "su", "sv", "v", "s", "t", "g")}
        self.sems = {k: nc.alloc_semaphore("sem_" + k) for k in self.cnt}

    def op(self, eng, fn, inc=None, inc_n=1, waits=()):
        for sname, val in waits:
            assert val <= self.cnt[sname], (eng, sname, val, self.cnt[sname])
        if inc is not None:
            self.cnt[inc] += inc_n
        sems = self.sems
        isem = sems[inc] if inc is not None else None

        def run(e, fn=fn, waits=tuple(waits), isem=isem, n=inc_n):
            for sname, val in waits:
                e.wait_ge(sems[sname], val)
            ins = fn(e)
            if isem is not None:
                ins.then_inc(isem, n)

        self.ops[eng].append(run)
        return self.cnt[inc] if inc is not None else None

    def selfsync(self, eng, sem):
        val = self.cnt[sem]
        sems = self.sems

        def run(e, val=val, sem=sem):
            e.wait_ge(sems[sem], val)

        self.ops[eng].append(run)


def build():
    nc = bass.Bass("TRN2", target_bir_lowering=False)

    x_d = nc.dram_tensor("x", [S, D], F32, kind="ExternalInput")
    wa_d = nc.dram_tensor("wa", [HN, D], F32, kind="ExternalInput")
    wi_d = nc.dram_tensor("wi", [HN, D], F32, kind="ExternalInput")
    wo_d = nc.dram_tensor("wo", [D, HN], F32, kind="ExternalInput")
    ba_d = nc.dram_tensor("ba", [1, HN], F32, kind="ExternalInput")
    bi_d = nc.dram_tensor("bi", [1, HN], F32, kind="ExternalInput")
    bo_d = nc.dram_tensor("bo", [1, D], F32, kind="ExternalInput")
    gm_d = nc.dram_tensor("gm", [1, D], F32, kind="ExternalInput")
    bt_d = nc.dram_tensor("bt", [1, D], F32, kind="ExternalInput")
    w1_d = nc.dram_tensor("w1", [FF, D], F32, kind="ExternalInput")
    b1_d = nc.dram_tensor("b1v", [KH, 128], F32, kind="ExternalInput")
    w2_d = nc.dram_tensor("w2", [D, FF], F32, kind="ExternalInput")
    b2_d = nc.dram_tensor("b2v", [1, D], F32, kind="ExternalInput")
    out_d = nc.dram_tensor("out", [S, D], F32, kind="ExternalOutput")

    w1T = nc.alloc_sbuf_tensor("w1T", [128, KD, KH, 128], BF16)
    w2T = nc.alloc_sbuf_tensor("w2T", [128, KH, 1024], BF16)
    waT_bf = nc.alloc_sbuf_tensor("waT_bf", [128, KD, HN], BF16)
    winT_bf = nc.alloc_sbuf_tensor("winT_bf", [128, KD, HN], BF16)
    woutT_bf = nc.alloc_sbuf_tensor("woutT_bf", [HN + 1, D], BF16)
    gamma_f = nc.alloc_sbuf_tensor("gamma_f", [128, KD], F32)
    beta_f = nc.alloc_sbuf_tensor("beta_f", [128, KD], F32)
    b1_col = nc.alloc_sbuf_tensor("b1_col", [128, KH], F32)
    ba_col = nc.alloc_sbuf_tensor("ba_col", [HN, 2], F32)
    bab_p = nc.alloc_sbuf_tensor("bab_p", [HN, 2], F32)
    b2_bc = nc.alloc_sbuf_tensor("b2_bc", [128, D], F32)
    ident_b = nc.alloc_sbuf_tensor("ident_b", [128, 128], BF16)
    ones_f = nc.alloc_sbuf_tensor("ones_f", [1, 128], F32)

    x_sbA = nc.alloc_sbuf_tensor("x_sbA", [128, NT, D], F32)
    x_sbB = nc.alloc_sbuf_tensor("x_sbB", [128, NT, D], F32)
    x_sbC = nc.alloc_sbuf_tensor("x_sbC", [128, NT, D], F32)
    xnf_bfA = nc.alloc_sbuf_tensor("xnf_bfA", [128, KD, T], BF16)
    xnf_bfB = nc.alloc_sbuf_tensor("xnf_bfB", [128, KD, T], BF16)
    uf_bf = nc.alloc_sbuf_tensor("uf_bf", [128, KD, T], BF16)
    h1_bf = nc.alloc_sbuf_tensor("h1_bf", [128, KH, T], BF16)
    g_sb = nc.alloc_sbuf_tensor("g_sb", [128, D], F32)
    s_sb = nc.alloc_sbuf_tensor("s_sb", [128, D], F32)
    stage = nc.alloc_sbuf_tensor("stage", [128, 4 * D], BF16)
    a_s = nc.alloc_sbuf_tensor("a_s", [HN, T], F32)
    b_s = nc.alloc_sbuf_tensor("b_s", [HN, T], F32)
    h_s = nc.alloc_sbuf_tensor("h_s", [HN, T], F32)
    h_bf = nc.alloc_sbuf_tensor("h_bf", [HN + 1, T], BF16)
    carry = nc.alloc_sbuf_tensor("carry", [HN, 1], F32)
    st6 = nc.alloc_sbuf_tensor("st6", [128, 12], F32)
    mv_all = nc.alloc_sbuf_tensor("mv_all", [128, 4, NT, 2], F32)
    r1_all = nc.alloc_sbuf_tensor("r1_all", [128, 4, NT, 1], F32)
    r12_all = nc.alloc_sbuf_tensor("r12_all", [128, NT], F32)
    veps = nc.alloc_sbuf_tensor("veps", [128, NT], F32)
    stdt = nc.alloc_sbuf_tensor("stdt", [128, 2 * NT], F32)
    w2t = nc.alloc_sbuf_tensor("w2t", [128, NT], F32)
    pw = veps
    alph = nc.alloc_sbuf_tensor("alph", [128, 6], F32)

    h1_flat = h1_bf[:].rearrange("p a b -> p (a b)")
    wstg = lambda a, b: h1_flat[:, a:b]
    x_flat = x_sbA[:].rearrange("p a b -> p (a b)")
    wa_st = lambda lo, j0, j1: x_flat[lo:lo + HN, j0:j1]
    waT_f = s_sb[:][:, 0:256].rearrange("p (a b) -> p a b", b=HN)
    winT_f = s_sb[:][:, 256:512].rearrange("p (a b) -> p a b", b=HN)
    wo_sb = g_sb[:][:, 0:256].rearrange("p (a b) -> p a b", b=HN)
    xB_flat = x_sbB[:].rearrange("p a b -> p (a b)")
    rows_sb = xB_flat[0:1, 0:D]
    xC_flat = x_sbC[:].rearrange("p a b -> p (a b)")
    ident_f = xC_flat[:, 0:128]
    b1v_sb = xC_flat[0:HN, 128:256]
    bab_sb = xC_flat[0:1, 256:256 + 2 * HN]

    ps_tr0 = nc.alloc_psum_tensor("ps_tr0", [128, 1024], BF16)
    ps_tr1 = nc.alloc_psum_tensor("ps_tr1", [128, 1024], BF16)
    ps_zab = nc.alloc_psum_tensor("ps_zab", [HN, 512], F32)
    ps_h1a = nc.alloc_psum_tensor("ps_h1a", [128, 512], F32)
    ps_h1b = nc.alloc_psum_tensor("ps_h1b", [128, 512], F32)
    ps_ga = nc.alloc_psum_tensor("ps_ga", [128, 512], F32)
    ps_gb = nc.alloc_psum_tensor("ps_gb", [128, 512], F32)
    ps_ms = nc.alloc_psum_tensor("ps_ms", [128, 512], F32)

    em = Emitter(nc)
    opx = em.op
    C = em.cnt

    opx("gpsimd", lambda e: e.memset(ident_f, 0.0), inc="g")
    g_if0 = C["g"]
    opx("gpsimd", lambda e: e.affine_select(
        out=ident_f, in_=ident_f, compare_op=ALU.not_equal, fill=1.0,
        base=0, pattern=[[-1, 128]], channel_multiplier=1),
        inc="g", waits=[("g", g_if0)])
    opx("gpsimd", lambda e: e.memset(ident_b[:], 0.0), inc="g")
    g_ib0 = C["g"]
    opx("gpsimd", lambda e: e.affine_select(
        out=ident_b[:], in_=ident_b[:], compare_op=ALU.not_equal, fill=1.0,
        base=0, pattern=[[-1, 128]], channel_multiplier=1),
        inc="g", waits=[("g", g_ib0)])
    opx("gpsimd", lambda e: e.memset(ones_f[:], 1.0), inc="g")
    opx("gpsimd", lambda e: e.memset(h_bf[HN:HN + 1, :], 1.0), inc="g")
    g_setup = C["g"]

    opx("gpsimd", lambda e: e.dma_start(wa_st(0, 0, D), wa_d[:]), inc="wl", inc_n=16)
    opx("gpsimd", lambda e: e.dma_start(
        wo_sb, wo_d[:].rearrange("(a p) c -> p a c", p=128)), inc="wl", inc_n=16)
    opx("gpsimd", lambda e: e.dma_start(bab_sb[0:1, 0:HN], ba_d[:]), inc="wl", inc_n=16)
    opx("gpsimd", lambda e: e.dma_start(bab_sb[0:1, HN:2 * HN], bi_d[:]), inc="wl", inc_n=16)
    opx("gpsimd", lambda e: e.dma_start(rows_sb, gm_d[:]), inc="wl", inc_n=16)
    opx("gpsimd", lambda e: e.dma_start(b1v_sb, b1_d[:]), inc="wl", inc_n=16)
    opx("gpsimd", lambda e: e.dma_start(woutT_bf[HN:HN + 1, :], bo_d[:]), inc="wl", inc_n=16)
    wl_small = C["wl"]

    opx("vector", lambda e: e.memset(carry[:], 0.0), inc="v")

    for j in range(KD):
        opx("tensor", lambda e, j=j: e.transpose(
            ps_ms[:, j * HN:(j + 1) * HN], wa_st(0, j * 128, (j + 1) * 128),
            ident_f[0:HN, 0:HN]),
            waits=[("wl", wl_small), ("g", g_setup)] if j == 0 else (),
            inc="t" if j == KD - 1 else None)
    t_r1a = C["t"]
    opx("vector", lambda e: e.tensor_copy(
        waT_f, ps_ms[:, 0:256].rearrange("p (a b) -> p a b", b=HN)),
        waits=[("t", t_r1a)], inc="v")
    v_r1a = C["v"]
    opx("gpsimd", lambda e: e.dma_start(wa_st(0, 0, D), wi_d[:]),
        inc="wl", inc_n=16, waits=[("t", t_r1a)])
    wl_wi = C["wl"]
    for j in range(KD):
        opx("tensor", lambda e, j=j: e.transpose(
            ps_ms[:, 256 + j * HN:256 + (j + 1) * HN],
            wa_st(0, j * 128, (j + 1) * 128), ident_f[0:HN, 0:HN]),
            waits=[("wl", wl_wi)] if j == 0 else (),
            inc="t" if j == KD - 1 else None)
    t_r1 = C["t"]
    opx("vector", lambda e: e.tensor_copy(
        winT_f, ps_ms[:, 256:512].rearrange("p (a b) -> p a b", b=HN)),
        waits=[("t", t_r1)], inc="v")
    v_r1 = C["v"]

    for j in range(KD):
        opx("tensor", lambda e, j=j: e.transpose(
            ps_ms[:, j:j + 1], rows_sb[:, j * 128:(j + 1) * 128],
            ident_f[0:1, 0:1]),
            waits=[("v", v_r1)] if j == 0 else ())
    opx("tensor", lambda e: e.transpose(
        ps_ms[:, 16:48], b1v_sb, ident_f[0:HN, 0:HN]))
    opx("tensor", lambda e: e.transpose(
        ps_ms[0:HN, 48:49], bab_sb[0:1, 0:HN], ident_f[0:1, 0:1]))
    opx("tensor", lambda e: e.transpose(
        ps_ms[0:HN, 49:50], bab_sb[0:1, HN:2 * HN], ident_f[0:1, 0:1]), inc="t")
    t_r2a = C["t"]
    opx("vector", lambda e: e.tensor_copy(gamma_f[:], ps_ms[:, 0:8]),
        waits=[("t", t_r2a)])
    opx("vector", lambda e: e.tensor_copy(b1_col[:], ps_ms[:, 16:48]))
    opx("vector", lambda e: e.tensor_copy(ba_col[:], ps_ms[0:HN, 48:50]), inc="v")
    v_r2a = C["v"]
    em.selfsync("vector", "v")
    opx("gpsimd", lambda e: e.dma_start(rows_sb, bt_d[:]),
        inc="wl", inc_n=16, waits=[("t", t_r2a)])
    wl_bt = C["wl"]
    for j in range(KD):
        opx("tensor", lambda e, j=j: e.transpose(
            ps_ms[:, 8 + j:9 + j], rows_sb[:, j * 128:(j + 1) * 128],
            ident_f[0:1, 0:1]),
            waits=[("wl", wl_bt), ("v", v_r2a)] if j == 0 else (),
            inc="t" if j == KD - 1 else None)
    t_r2b = C["t"]
    opx("vector", lambda e: e.tensor_copy(beta_f[:], ps_ms[:, 8:16]),
        waits=[("t", t_r2b)], inc="v")
    v_r2 = C["v"]
    em.selfsync("vector", "v")
    opx("gpsimd", lambda e: e.dma_start(rows_sb, b2_d[:]),
        inc="wl", inc_n=16, waits=[("t", t_r2b)])
    wl_b2 = C["wl"]

    v_evac_r3 = v_r2
    for h in range(2):
        for j in range(4):
            a = h * 4 + j
            opx("tensor", lambda e, a=a, j=j: e.transpose(
                ps_ms[0:HN, j * 128:(j + 1) * 128], wo_sb[:, a, :], ident_f),
                waits=[("v", v_evac_r3)] if j == 0 else (),
                inc="t" if j == 3 else None)
        t_r3 = C["t"]
        opx("vector", lambda e, h=h: e.tensor_copy(
            woutT_bf[0:HN, h * 512:(h + 1) * 512], ps_ms[0:HN, 0:512]),
            waits=[("t", t_r3)], inc="v")
        v_evac_r3 = C["v"]
    v_r3 = C["v"]

    for j in range(KD):
        opx("vector", lambda e, j=j: e.tensor_scalar_mul(
            waT_bf[:, j, :], waT_f[:, j, :], gamma_f[:, j:j + 1]))
    for j in range(KD):
        opx("vector", lambda e, j=j: e.tensor_scalar_mul(
            winT_bf[:, j, :], winT_f[:, j, :], gamma_f[:, j:j + 1]),
            inc="v" if j == KD - 1 else None)
    v_r4 = C["v"]

    for j in range(KD):
        opx("tensor", lambda e, j=j: e.matmul(
            ps_ms[0:HN, 0:1], waT_f[:, j, :], beta_f[:, j:j + 1],
            start=(j == 0), stop=(j == KD - 1)),
            waits=[("v", v_r3)] if j == 0 else ())
    for j in range(KD):
        opx("tensor", lambda e, j=j: e.matmul(
            ps_ms[0:HN, 1:2], winT_f[:, j, :], beta_f[:, j:j + 1],
            start=(j == 0), stop=(j == KD - 1)),
            inc="t" if j == KD - 1 else None)
    t_r5 = C["t"]
    opx("vector", lambda e: e.tensor_add(bab_p[:], ps_ms[0:HN, 0:2], ba_col[:]),
        waits=[("t", t_r5)], inc="v")
    v_r5 = C["v"]

    opx("tensor", lambda e: e.matmul(
        ps_ms[:, 0:512], ones_f[0:1, :], rows_sb[:, 0:512],
        start=True, stop=True),
        waits=[("v", v_r5), ("wl", wl_b2)], inc="t")
    t_b2a = C["t"]
    opx("vector", lambda e: e.tensor_copy(b2_bc[:, 0:512], ps_ms[:, 0:512]),
        waits=[("t", t_b2a)], inc="v")
    v_b2a = C["v"]
    opx("tensor", lambda e: e.matmul(
        ps_ms[:, 0:512], ones_f[0:1, :], rows_sb[:, 512:1024],
        start=True, stop=True),
        waits=[("v", v_b2a)], inc="t")
    t_b2b = C["t"]
    opx("vector", lambda e: e.tensor_copy(b2_bc[:, 512:1024], ps_ms[:, 0:512]),
        waits=[("t", t_b2b)], inc="v")
    v_b2_done = C["v"]


    ps_tr = [ps_tr0, ps_tr1]
    tr_free = {0: ("v", 0), 1: ("v", 0)}
    tr_fill = {0: 0, 1: 0}

    def trans8(bank, src_slices, dst_ap_fn, extra_waits, evac_eng="vector"):
        ps = ps_tr[bank]
        wts = list(extra_waits) + [tr_free[bank]]
        for k, src in enumerate(src_slices):
            opx("tensor", lambda e, k=k, src=src, ps=ps: e.transpose(
                ps[:, k * 128:(k + 1) * 128], src, ident_b[:]),
                waits=wts if k == 0 else (), inc="t" if k == len(src_slices) - 1 else None)
        tr_fill[bank] = C["t"]
        if evac_eng == "vector":
            opx("vector", lambda e, ps=ps: e.tensor_copy(
                dst_ap_fn(), ps[:].rearrange("p (a b) -> p a b", b=128)),
                waits=[("t", tr_fill[bank])], inc="v")
            tr_free[bank] = ("v", C["v"])
        else:
            opx("scalar", lambda e, ps=ps: e.copy(
                dst_ap_fn(), ps[:].rearrange("p (a b) -> p a b", b=128)),
                waits=[("t", tr_fill[bank])], inc="s")
            tr_free[bank] = ("s", C["s"])
        return tr_fill[bank]

    w1_fill = [0] * KH
    wsems = ["wd", "we", "w2", "w3", "w4", "w5", "w6", "w7"]

    def w1_part(i):
        o = (i % 8) * 1024
        a = (i % 3) * 2
        psem = wsems[i % 8]
        wl_i = C[psem]
        opx("vector", lambda e, o=o, a=a: e.reduce_sum(
            alph[:, a:a + 1], wstg(o, o + 1024), AX.X, apply_absolute_value=True),
            waits=[(psem, wl_i)], inc="v")
        em.selfsync("vector", "v")
        opx("vector", lambda e, a=a: e.tensor_scalar_mul(
            alph[:, a + 1:a + 2], alph[:, a:a + 1], 1.0 / D), inc="v")
        v_alpha = C["v"]
        em.selfsync("vector", "v")
        opx("scalar", lambda e, o=o: e.sign(wstg(o, o + 1024), wstg(o, o + 1024)),
            waits=[("v", v_alpha)], inc="s")
        return C["s"]

    def w1_tail(i, s_sign):
        o = (i % 8) * 1024
        a = (i % 3) * 2
        opx("vector", lambda e, o=o, a=a: e.tensor_scalar_mul(
            wstg(o, o + 1024), wstg(o, o + 1024), alph[:, a + 1:a + 2]),
            waits=[("s", s_sign)], inc="v")
        v_fold = C["v"]
        srcs = [wstg(o + k * 128, o + (k + 1) * 128) for k in range(KD)]
        w1_fill[i] = trans8(
            i % 2, srcs, (lambda i=i: w1T[:, :, i, :]), [("v", v_fold)],
            evac_eng="scalar" if i % 2 == 1 else "vector")

    w1_sign = {}
    for i in range(KH):
        wts = [("t", w1_fill[i - 8])] if i >= 8 else []
        psem = wsems[i % 8]
        opx("gpsimd", lambda e, i=i: e.dma_start(
            wstg((i % 8) * 1024, (i % 8) * 1024 + 1024),
            w1_d[i * 128:(i + 1) * 128, :]),
            inc=psem, inc_n=16, waits=wts)
        w1_sign[i] = w1_part(i)
        if i >= 1:
            w1_tail(i - 1, w1_sign[i - 1])
    w1_tail(KH - 1, w1_sign[KH - 1])

    w2_fill = [0] * KD
    w2_sign = {}

    def w2_round(i, r):
        o = (i % 2) * 4096
        srcs = [wstg(o + (r * 8 + j) * 128, o + (r * 8 + j + 1) * 128)
                for j in range(8)]
        w2_fill[i] = trans8(
            r % 2, srcs,
            (lambda i=i, r=r: w2T[:, r * 8:(r + 1) * 8, i * 128:(i + 1) * 128]),
            [("v", w2_fold[i])] if r == 0 else [],
            evac_eng="scalar")

    w2_fold = {}
    for i in range(KD):
        o = (i % 2) * 4096
        if i == 0:
            wts = [("t", w1_fill[27])]
        elif i == 1:
            wts = [("t", w1_fill[31])]
        else:
            wts = [("t", w2_fill[i - 2])]
        psem = wsems[i % 2]
        opx("gpsimd", lambda e, i=i, o=o: e.dma_start(
            wstg(o, o + 4096), w2_d[i * 128:(i + 1) * 128, :]),
            inc=psem, inc_n=16, waits=wts)
        wl_i = C[psem]
        for r in range(4):
            opx("vector", lambda e, o=o, r=r: e.reduce_sum(
                stdt[:, r:r + 1], wstg(o + r * 1024, o + (r + 1) * 1024),
                AX.X, apply_absolute_value=True),
                waits=[(psem, wl_i)] if r == 0 else (),
                inc="v" if r == 3 else None)
            if i >= 1:
                w2_round(i - 1, r)
        em.selfsync("vector", "v")
        a = (i % 3) * 2
        opx("vector", lambda e, a=a: e.tensor_add(
            alph[:, a:a + 1], stdt[:, 0:1], stdt[:, 1:2]))
        opx("vector", lambda e: e.tensor_add(
            stdt[:, 2:3], stdt[:, 2:3], stdt[:, 3:4]), inc="v")
        em.selfsync("vector", "v")
        opx("vector", lambda e, a=a: e.tensor_add(
            alph[:, a:a + 1], alph[:, a:a + 1], stdt[:, 2:3]), inc="v")
        em.selfsync("vector", "v")
        opx("vector", lambda e, a=a: e.tensor_scalar_mul(
            alph[:, a + 1:a + 2], alph[:, a:a + 1], 1.0 / FF), inc="v")
        v_alpha = C["v"]
        em.selfsync("vector", "v")
        opx("scalar", lambda e, o=o: e.sign(wstg(o, o + 4096), wstg(o, o + 4096)),
            waits=[("v", v_alpha)], inc="s")
        s_sign = C["s"]
        opx("vector", lambda e, o=o, a=a: e.tensor_scalar_mul(
            wstg(o, o + 4096), wstg(o, o + 4096), alph[:, a + 1:a + 2]),
            waits=[("s", s_sign)], inc="v")
        w2_fold[i] = C["v"]
    for r in range(4):
        w2_round(KD - 1, r)

    prologue_end = dict(C)

    x_par = [x_sbA, x_sbB, x_sbC]
    ldsem = ["ld", "le", "lf"]
    xload_cnt = {}
    stage_free = {j: 0 for j in range(4)}
    PP = dict(vU=[0, 0], vXN=[0, 0], tZ=0, sZ=0, vB=0, vScan=0,
              gelu_tail=[0, 0, 0], g_free=0, lnS=0, uf_ev=("v", 0),
              xnf_ev=("v", 0), vS_last=0)
    xnf_par = [xnf_bfA, xnf_bfB]

    def emit_trans_group(cc, slot):
        t = slot // 2
        is_xn = slot % 2
        pcc = cc % 2
        if is_xn:
            stage_free[slot] = trans8(
                1, [stage[:, slot * D + k * 128:slot * D + (k + 1) * 128]
                    for k in range(KD)],
                (lambda t=t, pcc=pcc: xnf_par[pcc][:, :, t * 128:(t + 1) * 128]),
                [("v", PP["vXN"][t])])
            PP["xnf_ev"] = tr_free[1]
        else:
            stage_free[slot] = trans8(
                0, [stage[:, slot * D + k * 128:slot * D + (k + 1) * 128]
                    for k in range(KD)],
                (lambda t=t: uf_bf[:, :, t * 128:(t + 1) * 128]),
                [("v", PP["vU"][t])])
            PP["uf_ev"] = tr_free[0]
    gelu_done = [0] * KH

    def emit_xload(c):
        p = c % 3
        if c == 0:
            wts = [("t", t_r1)]
        elif c == 1:
            wts = [("t", t_b2b)]
        elif c == 2:
            wts = [("t", t_r3)]
        else:
            wts = [(("st", "su", "sv")[c % 3], 16 * ((c - 3) // 3 + 1))]
        opx("sync", lambda e, c=c, p=p: e.dma_start(
            x_par[p][:], x_d[c * T:(c + 1) * T, :].rearrange(
                "(t p) d -> p t d", p=128)),
            inc=ldsem[p], inc_n=16, waits=wts)
        xload_cnt[c] = C[ldsem[p]]

    def emit_ln_ph1(cc):
        p = cc % 4
        xt = x_par[cc % 3]
        vA = [0, 0]
        for t in range(NT):
            wts = []
            if t == 0:
                wts = [(ldsem[cc % 3], xload_cnt[cc]), ("s", PP["lnS"])]
            opx("vector", lambda e, t=t, xt=xt: e.bn_stats(
                st6[:, 0:6], xt[:, t, 0:512]), waits=wts)
            opx("vector", lambda e, t=t, xt=xt: e.bn_stats(
                st6[:, 6:12], xt[:, t, 512:1024]), inc="v")
            em.selfsync("vector", "v")
            opx("vector", lambda e, t=t, p=p: e.bn_aggr(
                mv_all[:, p, t, :], st6[:]), inc="v")
            em.selfsync("vector", "v")
            opx("vector", lambda e, t=t, p=p: e.tensor_scalar_add(
                veps[:, t:t + 1], mv_all[:, p, t, 1:2], EPS), inc="v")
            vA[t] = C["v"]
        return vA

    def emit_ln_sqrt1(cc, vA):
        sA = [0, 0]
        for t in range(NT):
            opx("scalar", lambda e, t=t: e.sqrt(
                stdt[:, t:t + 1], veps[:, t:t + 1]),
                waits=[("v", vA[t])], inc="s")
            sA[t] = C["s"]
        return sA

    def emit_ln_ph2(cc, sA):
        p = cc % 4
        vW = [0, 0]
        for t in range(NT):
            opx("vector", lambda e, t=t, p=p: e.reciprocal(
                r1_all[:, p, t, :], stdt[:, t:t + 1]),
                waits=[("s", sA[t])], inc="v")
            em.selfsync("vector", "v")
            opx("vector", lambda e, t=t, p=p: e.tensor_mul(
                pw[:, t:t + 1], r1_all[:, p, t, :], r1_all[:, p, t, :]), inc="v")
            em.selfsync("vector", "v")
            opx("vector", lambda e, t=t, p=p: e.tensor_scalar(
                w2t[:, t:t + 1], pw[:, t:t + 1], mv_all[:, p, t, 1:2], EPS,
                ALU.mult, ALU.add), inc="v")
            vW[t] = C["v"]
        return vW

    def emit_ln_sqrt2(cc, vW):
        sB = [0, 0]
        for t in range(NT):
            opx("scalar", lambda e, t=t: e.sqrt(
                stdt[:, NT + t:NT + t + 1], w2t[:, t:t + 1]),
                waits=[("v", vW[t])], inc="s")
            sB[t] = C["s"]
        PP["lnS"] = C["s"]
        return sB

    def emit_ln_ph3(cc, sB):
        p = cc % 4
        xt = x_par[cc % 3]
        for t in range(NT):
            opx("vector", lambda e, t=t: e.reciprocal(
                pw[:, t:t + 1], stdt[:, NT + t:NT + t + 1]),
                waits=[("s", sB[t])], inc="v")
            em.selfsync("vector", "v")
            opx("vector", lambda e, t=t, p=p: e.tensor_mul(
                r12_all[:, t:t + 1], r1_all[:, p, t, :], pw[:, t:t + 1]), inc="v")
            em.selfsync("vector", "v")
            opx("vector", lambda e, t=t, p=p, xt=xt: e.tensor_scalar(
                stage[:, (2 * t) * D:(2 * t + 1) * D], xt[:, t, :],
                mv_all[:, p, t, 0:1], r12_all[:, t:t + 1],
                ALU.subtract, ALU.mult),
                waits=[("t", stage_free[2 * t])], inc="v")
            PP["vU"][t] = C["v"]
            opx("vector", lambda e, t=t, p=p, xt=xt: e.tensor_scalar(
                stage[:, (2 * t + 1) * D:(2 * t + 2) * D], xt[:, t, :],
                mv_all[:, p, t, 0:1], r1_all[:, p, t, :],
                ALU.subtract, ALU.mult),
                waits=[("t", stage_free[2 * t + 1])], inc="v")
            PP["vXN"][t] = C["v"]

    def emit_ln_block(cc):
        vA = emit_ln_ph1(cc)
        sA = emit_ln_sqrt1(cc, vA)
        vW = emit_ln_ph2(cc, sA)
        sB = emit_ln_sqrt2(cc, vW)
        emit_ln_ph3(cc, sB)

    emit_xload(0)
    emit_xload(1)
    emit_ln_block(0)
    for slot in range(4):
        emit_trans_group(0, slot)
    emit_ln_block(1)

    for c in range(NCH):
        p = c % 3
        xt = x_par[p]
        pf = c % 2
        p4 = c % 4
        if c + 2 < NCH:
            emit_xload(c + 2)

        uf_ev = PP["uf_ev"]
        xnf_ev = PP["xnf_ev"]
        xnf_c = xnf_par[pf]

        for k in range(KD):
            opx("tensor", lambda e, k=k: e.matmul(
                ps_zab[:, 0:T], waT_bf[:, k, :], uf_bf[:, k, :],
                start=(k == 0), stop=(k == KD - 1)),
                waits=[uf_ev, ("s", PP["sZ"]), ("v", PP["vB"])]
                if k == 0 else ())
        for k in range(KD):
            opx("tensor", lambda e, k=k: e.matmul(
                ps_zab[:, 256:256 + T], winT_bf[:, k, :], uf_bf[:, k, :],
                start=(k == 0), stop=(k == KD - 1)),
                inc="t" if k == KD - 1 else None)
        tZ = C["t"]

        opx("scalar", lambda e: e.activation(
            a_s[:], ps_zab[:, 0:T], AF.Sigmoid, bias=bab_p[:, 0:1]),
            waits=[("t", tZ), ("v", PP["vScan"])], inc="s")
        sZ = C["s"]
        opx("vector", lambda e: e.tensor_scalar_add(
            b_s[:], ps_zab[:, 256:256 + T], bab_p[:, 1:2]), waits=[("t", tZ)],
            inc="v")
        vB = C["v"]
        em.selfsync("vector", "v")
        opx("vector", lambda e: e.tensor_tensor_scan(
            h_s[:], a_s[:], b_s[:], carry[:, 0:1], ALU.mult, ALU.add),
            waits=[("s", sZ)], inc="v")
        vScan = C["v"]
        em.selfsync("vector", "v")
        opx("vector", lambda e: e.tensor_copy(carry[:], h_s[:, T - 1:T]), inc="v")
        em.selfsync("vector", "v")
        opx("vector", lambda e: e.tensor_copy(h_bf[0:HN, :], h_s[:]), inc="v")
        vH = C["v"]

        h1_banks = [ps_h1a, ps_h1b, ps_ms]
        for m in range(KH):
            if c + 1 < NCH and m in (13, 16, 19, 22):
                emit_trans_group(c + 1, (m - 13) // 3)
            ps = h1_banks[m % 3]
            if m == 0:
                wts = [xnf_ev, ("s", PP["gelu_tail"][0]),
                       ("v", PP["vS_last"])]
            elif m in (1, 2):
                wts = [("s", PP["gelu_tail"][m])]
                if m == 2 and c == 0:
                    wts.append(("v", v_b2_done))
            elif m == 3:
                wts = [("s", gelu_done[0])]
            elif m % 2 == 1:
                wts = []
            else:
                wts = [("s", gelu_done[m - 2])]
            for k in range(KD):
                opx("tensor", lambda e, k=k, m=m, ps=ps, xnf_c=xnf_c: e.matmul(
                    ps[:, 0:T], w1T[:, k, m, :], xnf_c[:, k, :],
                    start=(k == 0), stop=(k == KD - 1)),
                    waits=wts if k == 0 else (),
                    inc="t" if k == KD - 1 else None)
            tM = C["t"]
            opx("scalar", lambda e, m=m, ps=ps: e.activation(
                h1_bf[:, m, :], ps[:, 0:T], AF.Gelu, bias=b1_col[:, m:m + 1]),
                waits=[("t", tM)], inc="s")
            gelu_done[m] = C["s"]
        PP["gelu_tail"] = [gelu_done[30], gelu_done[31], gelu_done[29]]

        do_ln = c + 2 < NCH
        if do_ln:
            ln_vA = emit_ln_ph1(c + 2)
            ln_sA = emit_ln_sqrt1(c + 2, ln_vA)

        vS = 0
        for t in range(NT):
            awts = [("v", vH), ("s", gelu_done[KH - 1])] if t == 0 else [("v", vS)]
            opx("tensor", lambda e, t=t: e.matmul(
                ps_h1a[:, 0:512], h_bf[:, t * 128:(t + 1) * 128],
                woutT_bf[:, 0:512], start=True, stop=True),
                waits=awts)
            opx("tensor", lambda e, t=t: e.matmul(
                ps_h1b[:, 0:512], h_bf[:, t * 128:(t + 1) * 128],
                woutT_bf[:, 512:1024], start=True, stop=True), inc="t")
            tAT = C["t"]
            wts = [("t", t_r5)] if (c == 0 and t == 0) else []
            opx("vector", lambda e, t=t, p4=p4, xt=xt: e.tensor_scalar(
                s_sb[:], xt[:, t, :], mv_all[:, p4, t, 0:1], r1_all[:, p4, t, :],
                ALU.subtract, ALU.mult), waits=wts, inc="v")
            em.selfsync("vector", "v")
            opx("vector", lambda e: e.tensor_add(
                s_sb[:, 0:512], ps_h1a[:, 0:512], s_sb[:, 0:512]),
                waits=[("t", tAT)])
            opx("vector", lambda e: e.tensor_add(
                s_sb[:, 512:1024], ps_h1b[:, 0:512], s_sb[:, 512:1024]), inc="v")
            vS = C["v"]
            if t == 0 and do_ln:
                ln_vW = emit_ln_ph2(c + 2, ln_sA)
                ln_sB = emit_ln_sqrt2(c + 2, ln_vW)
                emit_ln_ph3(c + 2, ln_sB)
            for k in range(KH):
                opx("tensor", lambda e, k=k, t=t: e.matmul(
                    ps_ga[:, 0:512], h1_bf[:, k, t * 128:(t + 1) * 128],
                    w2T[:, k, 0:512], start=(k == 0), stop=(k == KH - 1)),
                    waits=[("v", PP["g_free"]), ("s", gelu_done[KH - 1])]
                    if k == 0 else ())
                opx("tensor", lambda e, k=k, t=t: e.matmul(
                    ps_gb[:, 0:512], h1_bf[:, k, t * 128:(t + 1) * 128],
                    w2T[:, k, 512:1024], start=(k == 0), stop=(k == KH - 1)),
                    inc="t" if k == KH - 1 else None)
            tG = C["t"]
            opx("vector", lambda e: e.tensor_add(
                g_sb[:, 0:512], ps_ga[:, 0:512], b2_bc[:, 0:512]),
                waits=[("t", tG)])
            opx("vector", lambda e: e.tensor_add(
                g_sb[:, 512:1024], ps_gb[:, 0:512], b2_bc[:, 512:1024]), inc="v")
            vG = C["v"]
            PP["g_free"] = vG
            em.selfsync("vector", "v")
            opx("vector", lambda e: e.tensor_mul(s_sb[:], s_sb[:], g_sb[:]),
                inc="v")
            vM1 = C["v"]
            opx("scalar", lambda e: e.activation(g_sb[:], g_sb[:], AF.Sigmoid),
                waits=[("v", vM1)], inc="s")
            sSIL = C["s"]
            opx("vector", lambda e: e.tensor_mul(s_sb[:], s_sb[:], g_sb[:]),
                waits=[("s", sSIL)], inc="v")
            em.selfsync("vector", "v")
            opx("vector", lambda e, t=t, xt=xt: e.tensor_add(
                xt[:, t, :], xt[:, t, :], s_sb[:]), inc="v")
            em.selfsync("vector", "v")
        vOUT = C["v"]
        opx("gpsimd", lambda e, c=c, xt=xt: e.dma_start(
            out_d[c * T:(c + 1) * T, :].rearrange("(t p) d -> p t d", p=128),
            xt[:]),
            waits=[("v", vOUT)], inc=("st", "su", "sv")[c % 3], inc_n=16)
        PP["tZ"] = tZ
        PP["sZ"] = sZ
        PP["vB"] = vB
        PP["vScan"] = vScan
        PP["vS_last"] = vS

    for snm in ("st", "su", "sv"):
        fv = C[snm]
        em.ops["sync"].append(lambda e, snm=snm, fv=fv: e.wait_ge(em.sems[snm], fv))

    with nc.Block() as block:

        @block.sync
        def _(e):
            for f in em.ops["sync"]:
                f(e)

        @block.gpsimd
        def _(e):
            for f in em.ops["gpsimd"]:
                f(e)

        @block.vector
        def _(e):
            for f in em.ops["vector"]:
                f(e)

        @block.scalar
        def _(e):
            for f in em.ops["scalar"]:
                f(e)

        @block.tensor
        def _(e):
            for f in em.ops["tensor"]:
                f(e)

    return nc


def kernel(L0_attn_x, L0_attn_W_a, L0_attn_b_a, L0_attn_W_in, L0_attn_b_in,
           L0_attn_W_out, L0_attn_b_out, L0_attn_ln_gamma, L0_attn_ln_beta,
           L0_ffn_w1, L0_ffn_b1, L0_ffn_w2, L0_ffn_b2):
    f = lambda a: np.ascontiguousarray(np.asarray(a, dtype=np.float32))
    x = f(L0_attn_x)
    shared = {
        "wa": f(L0_attn_W_a),
        "wi": f(L0_attn_W_in),
        "wo": f(L0_attn_W_out),
        "ba": f(L0_attn_b_a).reshape(1, HN),
        "bi": f(L0_attn_b_in).reshape(1, HN),
        "bo": f(L0_attn_b_out).reshape(1, D),
        "gm": f(L0_attn_ln_gamma).reshape(1, D),
        "bt": f(L0_attn_ln_beta).reshape(1, D),
        "w1": f(L0_ffn_w1),
        "b1v": f(L0_ffn_b1).reshape(KH, 128),
        "w2": f(L0_ffn_w2),
        "b2v": f(L0_ffn_b2).reshape(1, D),
    }
    nc = build()
    in_maps = [
        {k: np.copy(v) for k, v in dict(shared, x=x[b]).items()} for b in range(8)
    ]
    res = run_bass_kernel_spmd(nc, in_maps, core_ids=list(range(8)))
    return np.stack([res.results[b]["out"] for b in range(8)], axis=0)


# version 18
# speedup vs baseline: 1.2022x; 1.0075x over previous
import numpy as np
import concourse.bass as bass
import concourse.mybir as mybir
from concourse.bass_utils import run_bass_kernel_spmd

F32 = mybir.dt.float32
BF16 = mybir.dt.bfloat16
AF = mybir.ActivationFunctionType
ALU = mybir.AluOpType
AX = mybir.AxisListType

D = 1024
S = 4096
HN = 32
FF = 4096
T = 256
NT = T // 128
NCH = S // T
KD = D // 128
KH = FF // 128
EPS = 1e-5


class Emitter:
    def __init__(self, nc):
        self.nc = nc
        self.ops = {k: [] for k in ("sync", "gpsimd", "vector", "scalar", "tensor")}
        self.cnt = {k: 0 for k in ("ld", "wl", "wd", "we", "w2", "w3", "w4", "w5", "w6", "w7", "le", "lf", "st", "su", "sv", "v", "s", "t", "g")}
        self.sems = {k: nc.alloc_semaphore("sem_" + k) for k in self.cnt}

    def op(self, eng, fn, inc=None, inc_n=1, waits=()):
        for sname, val in waits:
            assert val <= self.cnt[sname], (eng, sname, val, self.cnt[sname])
        if inc is not None:
            self.cnt[inc] += inc_n
        sems = self.sems
        isem = sems[inc] if inc is not None else None

        def run(e, fn=fn, waits=tuple(waits), isem=isem, n=inc_n):
            for sname, val in waits:
                e.wait_ge(sems[sname], val)
            ins = fn(e)
            if isem is not None:
                ins.then_inc(isem, n)

        self.ops[eng].append(run)
        return self.cnt[inc] if inc is not None else None

    def selfsync(self, eng, sem):
        val = self.cnt[sem]
        sems = self.sems

        def run(e, val=val, sem=sem):
            e.wait_ge(sems[sem], val)

        self.ops[eng].append(run)


def build():
    nc = bass.Bass("TRN2", target_bir_lowering=False)

    x_d = nc.dram_tensor("x", [S, D], F32, kind="ExternalInput")
    wa_d = nc.dram_tensor("wa", [HN, D], F32, kind="ExternalInput")
    wi_d = nc.dram_tensor("wi", [HN, D], F32, kind="ExternalInput")
    wo_d = nc.dram_tensor("wo", [D, HN], F32, kind="ExternalInput")
    ba_d = nc.dram_tensor("ba", [1, HN], F32, kind="ExternalInput")
    bi_d = nc.dram_tensor("bi", [1, HN], F32, kind="ExternalInput")
    bo_d = nc.dram_tensor("bo", [1, D], F32, kind="ExternalInput")
    gm_d = nc.dram_tensor("gm", [1, D], F32, kind="ExternalInput")
    bt_d = nc.dram_tensor("bt", [1, D], F32, kind="ExternalInput")
    w1_d = nc.dram_tensor("w1", [FF, D], F32, kind="ExternalInput")
    b1_d = nc.dram_tensor("b1v", [KH, 128], F32, kind="ExternalInput")
    w2_d = nc.dram_tensor("w2", [D, FF], F32, kind="ExternalInput")
    b2_d = nc.dram_tensor("b2v", [1, D], F32, kind="ExternalInput")
    out_d = nc.dram_tensor("out", [S, D], F32, kind="ExternalOutput")

    w1T = nc.alloc_sbuf_tensor("w1T", [128, KD, KH, 128], BF16)
    w2T = nc.alloc_sbuf_tensor("w2T", [128, KH, 1024], BF16)
    waT_bf = nc.alloc_sbuf_tensor("waT_bf", [128, KD, HN], BF16)
    winT_bf = nc.alloc_sbuf_tensor("winT_bf", [128, KD, HN], BF16)
    woutT_bf = nc.alloc_sbuf_tensor("woutT_bf", [HN + 1, D], BF16)
    gamma_f = nc.alloc_sbuf_tensor("gamma_f", [128, KD], F32)
    beta_f = nc.alloc_sbuf_tensor("beta_f", [128, KD], F32)
    b1_col = nc.alloc_sbuf_tensor("b1_col", [128, KH], F32)
    ba_col = nc.alloc_sbuf_tensor("ba_col", [HN, 2], F32)
    bab_p = nc.alloc_sbuf_tensor("bab_p", [HN, 2], F32)
    b2_bc = nc.alloc_sbuf_tensor("b2_bc", [128, D], F32)
    ident_b = nc.alloc_sbuf_tensor("ident_b", [128, 128], BF16)
    ones_f = nc.alloc_sbuf_tensor("ones_f", [1, 128], F32)

    x_sbA = nc.alloc_sbuf_tensor("x_sbA", [128, NT, D], F32)
    x_sbB = nc.alloc_sbuf_tensor("x_sbB", [128, NT, D], F32)
    x_sbC = nc.alloc_sbuf_tensor("x_sbC", [128, NT, D], F32)
    xnf_bfA = nc.alloc_sbuf_tensor("xnf_bfA", [128, KD, T], BF16)
    xnf_bfB = nc.alloc_sbuf_tensor("xnf_bfB", [128, KD, T], BF16)
    uf_bf = nc.alloc_sbuf_tensor("uf_bf", [128, KD, T], BF16)
    h1_bf = nc.alloc_sbuf_tensor("h1_bf", [128, KH, T], BF16)
    g_sb = nc.alloc_sbuf_tensor("g_sb", [128, D], F32)
    s_sb = nc.alloc_sbuf_tensor("s_sb", [128, D], F32)
    stage = nc.alloc_sbuf_tensor("stage", [128, 4 * D], BF16)
    a_s = nc.alloc_sbuf_tensor("a_s", [HN, T], F32)
    b_s = nc.alloc_sbuf_tensor("b_s", [HN, T], F32)
    h_s = nc.alloc_sbuf_tensor("h_s", [HN, T], F32)
    h_bf = nc.alloc_sbuf_tensor("h_bf", [HN + 1, T], BF16)
    carry = nc.alloc_sbuf_tensor("carry", [HN, 1], F32)
    st6 = nc.alloc_sbuf_tensor("st6", [128, 12], F32)
    mv_all = nc.alloc_sbuf_tensor("mv_all", [128, 4, NT, 2], F32)
    r1_all = nc.alloc_sbuf_tensor("r1_all", [128, 4, NT, 1], F32)
    r12_all = nc.alloc_sbuf_tensor("r12_all", [128, NT], F32)
    veps = nc.alloc_sbuf_tensor("veps", [128, NT], F32)
    stdt = nc.alloc_sbuf_tensor("stdt", [128, 2 * NT], F32)
    w2t = nc.alloc_sbuf_tensor("w2t", [128, NT], F32)
    pw = veps
    alph = nc.alloc_sbuf_tensor("alph", [128, 6], F32)

    h1_flat = h1_bf[:].rearrange("p a b -> p (a b)")
    wstg = lambda a, b: h1_flat[:, a:b]
    x_flat = x_sbA[:].rearrange("p a b -> p (a b)")
    wa_st = lambda lo, j0, j1: x_flat[lo:lo + HN, j0:j1]
    waT_f = s_sb[:][:, 0:256].rearrange("p (a b) -> p a b", b=HN)
    winT_f = s_sb[:][:, 256:512].rearrange("p (a b) -> p a b", b=HN)
    wo_sb = g_sb[:][:, 0:256].rearrange("p (a b) -> p a b", b=HN)
    xB_flat = x_sbB[:].rearrange("p a b -> p (a b)")
    rows_sb = xB_flat[0:1, 0:D]
    xC_flat = x_sbC[:].rearrange("p a b -> p (a b)")
    ident_f = xC_flat[:, 0:128]
    b1v_sb = xC_flat[0:HN, 128:256]
    bab_sb = xC_flat[0:1, 256:256 + 2 * HN]

    ps_tr0 = nc.alloc_psum_tensor("ps_tr0", [128, 1024], BF16)
    ps_tr1 = nc.alloc_psum_tensor("ps_tr1", [128, 1024], BF16)
    ps_zab = nc.alloc_psum_tensor("ps_zab", [HN, 512], F32)
    ps_h1a = nc.alloc_psum_tensor("ps_h1a", [128, 512], F32)
    ps_h1b = nc.alloc_psum_tensor("ps_h1b", [128, 512], F32)
    ps_ga = nc.alloc_psum_tensor("ps_ga", [128, 512], F32)
    ps_gb = nc.alloc_psum_tensor("ps_gb", [128, 512], F32)
    ps_ms = nc.alloc_psum_tensor("ps_ms", [128, 512], F32)

    em = Emitter(nc)
    opx = em.op
    C = em.cnt

    opx("gpsimd", lambda e: e.memset(ident_f, 0.0), inc="g")
    g_if0 = C["g"]
    opx("gpsimd", lambda e: e.affine_select(
        out=ident_f, in_=ident_f, compare_op=ALU.not_equal, fill=1.0,
        base=0, pattern=[[-1, 128]], channel_multiplier=1),
        inc="g", waits=[("g", g_if0)])
    opx("gpsimd", lambda e: e.memset(ident_b[:], 0.0), inc="g")
    g_ib0 = C["g"]
    opx("gpsimd", lambda e: e.affine_select(
        out=ident_b[:], in_=ident_b[:], compare_op=ALU.not_equal, fill=1.0,
        base=0, pattern=[[-1, 128]], channel_multiplier=1),
        inc="g", waits=[("g", g_ib0)])
    opx("gpsimd", lambda e: e.memset(ones_f[:], 1.0), inc="g")
    opx("gpsimd", lambda e: e.memset(h_bf[HN:HN + 1, :], 1.0), inc="g")
    g_setup = C["g"]

    opx("gpsimd", lambda e: e.dma_start(wa_st(0, 0, D), wa_d[:]), inc="wl", inc_n=16)
    opx("gpsimd", lambda e: e.dma_start(
        wo_sb, wo_d[:].rearrange("(a p) c -> p a c", p=128)), inc="wl", inc_n=16)
    opx("gpsimd", lambda e: e.dma_start(bab_sb[0:1, 0:HN], ba_d[:]), inc="wl", inc_n=16)
    opx("gpsimd", lambda e: e.dma_start(bab_sb[0:1, HN:2 * HN], bi_d[:]), inc="wl", inc_n=16)
    opx("gpsimd", lambda e: e.dma_start(rows_sb, gm_d[:]), inc="wl", inc_n=16)
    opx("gpsimd", lambda e: e.dma_start(b1v_sb, b1_d[:]), inc="wl", inc_n=16)
    opx("gpsimd", lambda e: e.dma_start(woutT_bf[HN:HN + 1, :], bo_d[:]), inc="wl", inc_n=16)
    wl_small = C["wl"]

    opx("vector", lambda e: e.memset(carry[:], 0.0), inc="v")

    for j in range(KD):
        opx("tensor", lambda e, j=j: e.transpose(
            ps_ms[:, j * HN:(j + 1) * HN], wa_st(0, j * 128, (j + 1) * 128),
            ident_f[0:HN, 0:HN]),
            waits=[("wl", wl_small), ("g", g_setup)] if j == 0 else (),
            inc="t" if j == KD - 1 else None)
    t_r1a = C["t"]
    opx("vector", lambda e: e.tensor_copy(
        waT_f, ps_ms[:, 0:256].rearrange("p (a b) -> p a b", b=HN)),
        waits=[("t", t_r1a)], inc="v")
    v_r1a = C["v"]
    opx("gpsimd", lambda e: e.dma_start(wa_st(0, 0, D), wi_d[:]),
        inc="wl", inc_n=16, waits=[("t", t_r1a)])
    wl_wi = C["wl"]
    for j in range(KD):
        opx("tensor", lambda e, j=j: e.transpose(
            ps_ms[:, 256 + j * HN:256 + (j + 1) * HN],
            wa_st(0, j * 128, (j + 1) * 128), ident_f[0:HN, 0:HN]),
            waits=[("wl", wl_wi)] if j == 0 else (),
            inc="t" if j == KD - 1 else None)
    t_r1 = C["t"]
    opx("vector", lambda e: e.tensor_copy(
        winT_f, ps_ms[:, 256:512].rearrange("p (a b) -> p a b", b=HN)),
        waits=[("t", t_r1)], inc="v")
    v_r1 = C["v"]

    for j in range(KD):
        opx("tensor", lambda e, j=j: e.transpose(
            ps_ms[:, j:j + 1], rows_sb[:, j * 128:(j + 1) * 128],
            ident_f[0:1, 0:1]),
            waits=[("v", v_r1)] if j == 0 else ())
    opx("tensor", lambda e: e.transpose(
        ps_ms[:, 16:48], b1v_sb, ident_f[0:HN, 0:HN]))
    opx("tensor", lambda e: e.transpose(
        ps_ms[0:HN, 48:49], bab_sb[0:1, 0:HN], ident_f[0:1, 0:1]))
    opx("tensor", lambda e: e.transpose(
        ps_ms[0:HN, 49:50], bab_sb[0:1, HN:2 * HN], ident_f[0:1, 0:1]), inc="t")
    t_r2a = C["t"]
    opx("vector", lambda e: e.tensor_copy(gamma_f[:], ps_ms[:, 0:8]),
        waits=[("t", t_r2a)])
    opx("vector", lambda e: e.tensor_copy(b1_col[:], ps_ms[:, 16:48]))
    opx("vector", lambda e: e.tensor_copy(ba_col[:], ps_ms[0:HN, 48:50]), inc="v")
    v_r2a = C["v"]
    em.selfsync("vector", "v")
    opx("gpsimd", lambda e: e.dma_start(rows_sb, bt_d[:]),
        inc="wl", inc_n=16, waits=[("t", t_r2a)])
    wl_bt = C["wl"]
    for j in range(KD):
        opx("tensor", lambda e, j=j: e.transpose(
            ps_ms[:, 8 + j:9 + j], rows_sb[:, j * 128:(j + 1) * 128],
            ident_f[0:1, 0:1]),
            waits=[("wl", wl_bt), ("v", v_r2a)] if j == 0 else (),
            inc="t" if j == KD - 1 else None)
    t_r2b = C["t"]
    opx("vector", lambda e: e.tensor_copy(beta_f[:], ps_ms[:, 8:16]),
        waits=[("t", t_r2b)], inc="v")
    v_r2 = C["v"]
    em.selfsync("vector", "v")
    opx("gpsimd", lambda e: e.dma_start(rows_sb, b2_d[:]),
        inc="wl", inc_n=16, waits=[("t", t_r2b)])
    wl_b2 = C["wl"]

    v_evac_r3 = v_r2
    for h in range(2):
        for j in range(4):
            a = h * 4 + j
            opx("tensor", lambda e, a=a, j=j: e.transpose(
                ps_ms[0:HN, j * 128:(j + 1) * 128], wo_sb[:, a, :], ident_f),
                waits=[("v", v_evac_r3)] if j == 0 else (),
                inc="t" if j == 3 else None)
        t_r3 = C["t"]
        opx("vector", lambda e, h=h: e.tensor_copy(
            woutT_bf[0:HN, h * 512:(h + 1) * 512], ps_ms[0:HN, 0:512]),
            waits=[("t", t_r3)], inc="v")
        v_evac_r3 = C["v"]
    v_r3 = C["v"]

    for j in range(KD):
        opx("vector", lambda e, j=j: e.tensor_scalar_mul(
            waT_bf[:, j, :], waT_f[:, j, :], gamma_f[:, j:j + 1]))
    for j in range(KD):
        opx("vector", lambda e, j=j: e.tensor_scalar_mul(
            winT_bf[:, j, :], winT_f[:, j, :], gamma_f[:, j:j + 1]),
            inc="v" if j == KD - 1 else None)
    v_r4 = C["v"]

    for j in range(KD):
        opx("tensor", lambda e, j=j: e.matmul(
            ps_ms[0:HN, 0:1], waT_f[:, j, :], beta_f[:, j:j + 1],
            start=(j == 0), stop=(j == KD - 1)),
            waits=[("v", v_r3)] if j == 0 else ())
    for j in range(KD):
        opx("tensor", lambda e, j=j: e.matmul(
            ps_ms[0:HN, 1:2], winT_f[:, j, :], beta_f[:, j:j + 1],
            start=(j == 0), stop=(j == KD - 1)),
            inc="t" if j == KD - 1 else None)
    t_r5 = C["t"]
    opx("vector", lambda e: e.tensor_add(bab_p[:], ps_ms[0:HN, 0:2], ba_col[:]),
        waits=[("t", t_r5)], inc="v")
    v_r5 = C["v"]

    opx("tensor", lambda e: e.matmul(
        ps_ms[:, 0:512], ones_f[0:1, :], rows_sb[:, 0:512],
        start=True, stop=True),
        waits=[("v", v_r5), ("wl", wl_b2)], inc="t")
    t_b2a = C["t"]
    opx("vector", lambda e: e.tensor_copy(b2_bc[:, 0:512], ps_ms[:, 0:512]),
        waits=[("t", t_b2a)], inc="v")
    v_b2a = C["v"]
    opx("tensor", lambda e: e.matmul(
        ps_ms[:, 0:512], ones_f[0:1, :], rows_sb[:, 512:1024],
        start=True, stop=True),
        waits=[("v", v_b2a)], inc="t")
    t_b2b = C["t"]
    opx("vector", lambda e: e.tensor_copy(b2_bc[:, 512:1024], ps_ms[:, 0:512]),
        waits=[("t", t_b2b)], inc="v")
    v_b2_done = C["v"]


    ps_tr = [ps_tr0, ps_tr1]
    tr_free = {0: ("v", 0), 1: ("v", 0)}
    tr_fill = {0: 0, 1: 0}

    def trans8(bank, src_slices, dst_ap_fn, extra_waits, evac_eng="vector"):
        ps = ps_tr[bank]
        wts = list(extra_waits) + [tr_free[bank]]
        for k, src in enumerate(src_slices):
            opx("tensor", lambda e, k=k, src=src, ps=ps: e.transpose(
                ps[:, k * 128:(k + 1) * 128], src, ident_b[:]),
                waits=wts if k == 0 else (), inc="t" if k == len(src_slices) - 1 else None)
        tr_fill[bank] = C["t"]
        if evac_eng == "vector":
            opx("vector", lambda e, ps=ps: e.tensor_copy(
                dst_ap_fn(), ps[:].rearrange("p (a b) -> p a b", b=128)),
                waits=[("t", tr_fill[bank])], inc="v")
            tr_free[bank] = ("v", C["v"])
        else:
            opx("scalar", lambda e, ps=ps: e.copy(
                dst_ap_fn(), ps[:].rearrange("p (a b) -> p a b", b=128)),
                waits=[("t", tr_fill[bank])], inc="s")
            tr_free[bank] = ("s", C["s"])
        return tr_fill[bank]

    w1_fill = [0] * KH
    wsems = ["wd", "we", "w2", "w3", "w4", "w5", "w6", "w7"]

    def w1_part(i):
        o = (i % 8) * 1024
        a = (i % 3) * 2
        psem = wsems[i % 8]
        wl_i = C[psem]
        opx("vector", lambda e, o=o, a=a: e.reduce_sum(
            alph[:, a:a + 1], wstg(o, o + 1024), AX.X, apply_absolute_value=True),
            waits=[(psem, wl_i)], inc="v")
        em.selfsync("vector", "v")
        opx("vector", lambda e, a=a: e.tensor_scalar_mul(
            alph[:, a + 1:a + 2], alph[:, a:a + 1], 1.0 / D), inc="v")
        v_alpha = C["v"]
        em.selfsync("vector", "v")
        opx("scalar", lambda e, o=o: e.sign(wstg(o, o + 1024), wstg(o, o + 1024)),
            waits=[("v", v_alpha)], inc="s")
        return C["s"]

    def w1_tail(i, s_sign):
        o = (i % 8) * 1024
        a = (i % 3) * 2
        opx("vector", lambda e, o=o, a=a: e.tensor_scalar_mul(
            wstg(o, o + 1024), wstg(o, o + 1024), alph[:, a + 1:a + 2]),
            waits=[("s", s_sign)], inc="v")
        v_fold = C["v"]
        srcs = [wstg(o + k * 128, o + (k + 1) * 128) for k in range(KD)]
        w1_fill[i] = trans8(
            i % 2, srcs, (lambda i=i: w1T[:, :, i, :]), [("v", v_fold)],
            evac_eng="vector" if i % 3 == 0 else "scalar")

    w1_sign = {}
    for i in range(KH):
        wts = [("t", w1_fill[i - 8])] if i >= 8 else []
        psem = wsems[i % 8]
        opx("gpsimd", lambda e, i=i: e.dma_start(
            wstg((i % 8) * 1024, (i % 8) * 1024 + 1024),
            w1_d[i * 128:(i + 1) * 128, :]),
            inc=psem, inc_n=16, waits=wts)
        w1_sign[i] = w1_part(i)
        if i >= 1:
            w1_tail(i - 1, w1_sign[i - 1])
    w1_tail(KH - 1, w1_sign[KH - 1])

    w2_fill = [0] * KD
    w2_sign = {}

    def w2_round(i, r):
        o = (i % 2) * 4096
        srcs = [wstg(o + (r * 8 + j) * 128, o + (r * 8 + j + 1) * 128)
                for j in range(8)]
        w2_fill[i] = trans8(
            r % 2, srcs,
            (lambda i=i, r=r: w2T[:, r * 8:(r + 1) * 8, i * 128:(i + 1) * 128]),
            [("v", w2_fold[i])] if r == 0 else [],
            evac_eng="scalar")

    w2_fold = {}
    for i in range(KD):
        o = (i % 2) * 4096
        if i == 0:
            wts = [("t", w1_fill[27])]
        elif i == 1:
            wts = [("t", w1_fill[31])]
        else:
            wts = [("t", w2_fill[i - 2])]
        psem = wsems[i % 2]
        opx("gpsimd", lambda e, i=i, o=o: e.dma_start(
            wstg(o, o + 4096), w2_d[i * 128:(i + 1) * 128, :]),
            inc=psem, inc_n=16, waits=wts)
        wl_i = C[psem]
        for r in range(4):
            opx("vector", lambda e, o=o, r=r: e.reduce_sum(
                stdt[:, r:r + 1], wstg(o + r * 1024, o + (r + 1) * 1024),
                AX.X, apply_absolute_value=True),
                waits=[(psem, wl_i)] if r == 0 else (),
                inc="v" if r == 3 else None)
            if i >= 1:
                w2_round(i - 1, r)
        em.selfsync("vector", "v")
        a = (i % 3) * 2
        opx("vector", lambda e, a=a: e.tensor_add(
            alph[:, a:a + 1], stdt[:, 0:1], stdt[:, 1:2]))
        opx("vector", lambda e: e.tensor_add(
            stdt[:, 2:3], stdt[:, 2:3], stdt[:, 3:4]), inc="v")
        em.selfsync("vector", "v")
        opx("vector", lambda e, a=a: e.tensor_add(
            alph[:, a:a + 1], alph[:, a:a + 1], stdt[:, 2:3]), inc="v")
        em.selfsync("vector", "v")
        opx("vector", lambda e, a=a: e.tensor_scalar_mul(
            alph[:, a + 1:a + 2], alph[:, a:a + 1], 1.0 / FF), inc="v")
        v_alpha = C["v"]
        em.selfsync("vector", "v")
        opx("scalar", lambda e, o=o: e.sign(wstg(o, o + 4096), wstg(o, o + 4096)),
            waits=[("v", v_alpha)], inc="s")
        s_sign = C["s"]
        opx("vector", lambda e, o=o, a=a: e.tensor_scalar_mul(
            wstg(o, o + 4096), wstg(o, o + 4096), alph[:, a + 1:a + 2]),
            waits=[("s", s_sign)], inc="v")
        w2_fold[i] = C["v"]
    for r in range(4):
        w2_round(KD - 1, r)

    prologue_end = dict(C)

    x_par = [x_sbA, x_sbB, x_sbC]
    ldsem = ["ld", "le", "lf"]
    xload_cnt = {}
    stage_free = {j: 0 for j in range(4)}
    PP = dict(vU=[0, 0], vXN=[0, 0], tZ=0, sZ=0, vB=0, vScan=0,
              gelu_tail=[0, 0, 0], g_free=0, lnS=0, uf_ev=("v", 0),
              xnf_ev=("v", 0), vS_last=0)
    xnf_par = [xnf_bfA, xnf_bfB]

    def emit_trans_group(cc, slot):
        t = slot // 2
        is_xn = slot % 2
        pcc = cc % 2
        if is_xn:
            stage_free[slot] = trans8(
                1, [stage[:, slot * D + k * 128:slot * D + (k + 1) * 128]
                    for k in range(KD)],
                (lambda t=t, pcc=pcc: xnf_par[pcc][:, :, t * 128:(t + 1) * 128]),
                [("v", PP["vXN"][t])])
            PP["xnf_ev"] = tr_free[1]
        else:
            stage_free[slot] = trans8(
                0, [stage[:, slot * D + k * 128:slot * D + (k + 1) * 128]
                    for k in range(KD)],
                (lambda t=t: uf_bf[:, :, t * 128:(t + 1) * 128]),
                [("v", PP["vU"][t])])
            PP["uf_ev"] = tr_free[0]
    gelu_done = [0] * KH

    def emit_xload(c):
        p = c % 3
        if c == 0:
            wts = [("t", t_r1)]
        elif c == 1:
            wts = [("t", t_b2b)]
        elif c == 2:
            wts = [("t", t_r3)]
        else:
            wts = [(("st", "su", "sv")[c % 3], 16 * ((c - 3) // 3 + 1))]
        opx("sync", lambda e, c=c, p=p: e.dma_start(
            x_par[p][:], x_d[c * T:(c + 1) * T, :].rearrange(
                "(t p) d -> p t d", p=128)),
            inc=ldsem[p], inc_n=16, waits=wts)
        xload_cnt[c] = C[ldsem[p]]

    def emit_ln_ph1(cc):
        p = cc % 4
        xt = x_par[cc % 3]
        vA = [0, 0]
        for t in range(NT):
            wts = []
            if t == 0:
                wts = [(ldsem[cc % 3], xload_cnt[cc]), ("s", PP["lnS"])]
            opx("vector", lambda e, t=t, xt=xt: e.bn_stats(
                st6[:, 0:6], xt[:, t, 0:512]), waits=wts)
            opx("vector", lambda e, t=t, xt=xt: e.bn_stats(
                st6[:, 6:12], xt[:, t, 512:1024]), inc="v")
            em.selfsync("vector", "v")
            opx("vector", lambda e, t=t, p=p: e.bn_aggr(
                mv_all[:, p, t, :], st6[:]), inc="v")
            em.selfsync("vector", "v")
            opx("vector", lambda e, t=t, p=p: e.tensor_scalar_add(
                veps[:, t:t + 1], mv_all[:, p, t, 1:2], EPS), inc="v")
            vA[t] = C["v"]
        return vA

    def emit_ln_sqrt1(cc, vA):
        sA = [0, 0]
        for t in range(NT):
            opx("scalar", lambda e, t=t: e.sqrt(
                stdt[:, t:t + 1], veps[:, t:t + 1]),
                waits=[("v", vA[t])], inc="s")
            sA[t] = C["s"]
        return sA

    def emit_ln_ph2(cc, sA):
        p = cc % 4
        vW = [0, 0]
        for t in range(NT):
            opx("vector", lambda e, t=t, p=p: e.reciprocal(
                r1_all[:, p, t, :], stdt[:, t:t + 1]),
                waits=[("s", sA[t])], inc="v")
            em.selfsync("vector", "v")
            opx("vector", lambda e, t=t, p=p: e.tensor_mul(
                pw[:, t:t + 1], r1_all[:, p, t, :], r1_all[:, p, t, :]), inc="v")
            em.selfsync("vector", "v")
            opx("vector", lambda e, t=t, p=p: e.tensor_scalar(
                w2t[:, t:t + 1], pw[:, t:t + 1], mv_all[:, p, t, 1:2], EPS,
                ALU.mult, ALU.add), inc="v")
            vW[t] = C["v"]
        return vW

    def emit_ln_sqrt2(cc, vW):
        sB = [0, 0]
        for t in range(NT):
            opx("scalar", lambda e, t=t: e.sqrt(
                stdt[:, NT + t:NT + t + 1], w2t[:, t:t + 1]),
                waits=[("v", vW[t])], inc="s")
            sB[t] = C["s"]
        PP["lnS"] = C["s"]
        return sB

    def emit_ln_ph3(cc, sB):
        p = cc % 4
        xt = x_par[cc % 3]
        for t in range(NT):
            opx("vector", lambda e, t=t: e.reciprocal(
                pw[:, t:t + 1], stdt[:, NT + t:NT + t + 1]),
                waits=[("s", sB[t])], inc="v")
            em.selfsync("vector", "v")
            opx("vector", lambda e, t=t, p=p: e.tensor_mul(
                r12_all[:, t:t + 1], r1_all[:, p, t, :], pw[:, t:t + 1]), inc="v")
            em.selfsync("vector", "v")
            opx("vector", lambda e, t=t, p=p, xt=xt: e.tensor_scalar(
                stage[:, (2 * t) * D:(2 * t + 1) * D], xt[:, t, :],
                mv_all[:, p, t, 0:1], r12_all[:, t:t + 1],
                ALU.subtract, ALU.mult),
                waits=[("t", stage_free[2 * t])], inc="v")
            PP["vU"][t] = C["v"]
            opx("vector", lambda e, t=t, p=p, xt=xt: e.tensor_scalar(
                stage[:, (2 * t + 1) * D:(2 * t + 2) * D], xt[:, t, :],
                mv_all[:, p, t, 0:1], r1_all[:, p, t, :],
                ALU.subtract, ALU.mult),
                waits=[("t", stage_free[2 * t + 1])], inc="v")
            PP["vXN"][t] = C["v"]

    def emit_ln_block(cc):
        vA = emit_ln_ph1(cc)
        sA = emit_ln_sqrt1(cc, vA)
        vW = emit_ln_ph2(cc, sA)
        sB = emit_ln_sqrt2(cc, vW)
        emit_ln_ph3(cc, sB)

    emit_xload(0)
    emit_xload(1)
    emit_ln_block(0)
    for slot in range(4):
        emit_trans_group(0, slot)
    emit_ln_block(1)

    for c in range(NCH):
        p = c % 3
        xt = x_par[p]
        pf = c % 2
        p4 = c % 4
        if c + 2 < NCH:
            emit_xload(c + 2)

        uf_ev = PP["uf_ev"]
        xnf_ev = PP["xnf_ev"]
        xnf_c = xnf_par[pf]

        for k in range(KD):
            opx("tensor", lambda e, k=k: e.matmul(
                ps_zab[:, 0:T], waT_bf[:, k, :], uf_bf[:, k, :],
                start=(k == 0), stop=(k == KD - 1)),
                waits=[uf_ev, ("s", PP["sZ"]), ("v", PP["vB"])]
                if k == 0 else ())
        for k in range(KD):
            opx("tensor", lambda e, k=k: e.matmul(
                ps_zab[:, 256:256 + T], winT_bf[:, k, :], uf_bf[:, k, :],
                start=(k == 0), stop=(k == KD - 1)),
                inc="t" if k == KD - 1 else None)
        tZ = C["t"]

        opx("scalar", lambda e: e.activation(
            a_s[:], ps_zab[:, 0:T], AF.Sigmoid, bias=bab_p[:, 0:1]),
            waits=[("t", tZ), ("v", PP["vScan"])], inc="s")
        sZ = C["s"]
        opx("vector", lambda e: e.tensor_scalar_add(
            b_s[:], ps_zab[:, 256:256 + T], bab_p[:, 1:2]), waits=[("t", tZ)],
            inc="v")
        vB = C["v"]
        em.selfsync("vector", "v")
        opx("vector", lambda e: e.tensor_tensor_scan(
            h_s[:], a_s[:], b_s[:], carry[:, 0:1], ALU.mult, ALU.add),
            waits=[("s", sZ)], inc="v")
        vScan = C["v"]
        em.selfsync("vector", "v")
        opx("vector", lambda e: e.tensor_copy(carry[:], h_s[:, T - 1:T]), inc="v")
        em.selfsync("vector", "v")
        opx("vector", lambda e: e.tensor_copy(h_bf[0:HN, :], h_s[:]), inc="v")
        vH = C["v"]

        h1_banks = [ps_h1a, ps_h1b, ps_ms]
        for m in range(KH):
            if c + 1 < NCH and m in (13, 16, 19, 22):
                emit_trans_group(c + 1, (m - 13) // 3)
            ps = h1_banks[m % 3]
            if m == 0:
                wts = [xnf_ev, ("s", PP["gelu_tail"][0]),
                       ("v", PP["vS_last"])]
            elif m in (1, 2):
                wts = [("s", PP["gelu_tail"][m])]
                if m == 2 and c == 0:
                    wts.append(("v", v_b2_done))
            else:
                wts = [("s", gelu_done[m - 3])]
            for k in range(KD):
                opx("tensor", lambda e, k=k, m=m, ps=ps, xnf_c=xnf_c: e.matmul(
                    ps[:, 0:T], w1T[:, k, m, :], xnf_c[:, k, :],
                    start=(k == 0), stop=(k == KD - 1)),
                    waits=wts if k == 0 else (),
                    inc="t" if k == KD - 1 else None)
            tM = C["t"]
            opx("scalar", lambda e, m=m, ps=ps: e.activation(
                h1_bf[:, m, :], ps[:, 0:T], AF.Gelu, bias=b1_col[:, m:m + 1]),
                waits=[("t", tM)], inc="s")
            gelu_done[m] = C["s"]
        PP["gelu_tail"] = [gelu_done[30], gelu_done[31], gelu_done[29]]

        do_ln = c + 2 < NCH
        if do_ln:
            ln_vA = emit_ln_ph1(c + 2)
            ln_sA = emit_ln_sqrt1(c + 2, ln_vA)

        vS = 0
        for t in range(NT):
            awts = [("v", vH), ("s", gelu_done[KH - 1])] if t == 0 else [("v", vS)]
            opx("tensor", lambda e, t=t: e.matmul(
                ps_h1a[:, 0:512], h_bf[:, t * 128:(t + 1) * 128],
                woutT_bf[:, 0:512], start=True, stop=True),
                waits=awts)
            opx("tensor", lambda e, t=t: e.matmul(
                ps_h1b[:, 0:512], h_bf[:, t * 128:(t + 1) * 128],
                woutT_bf[:, 512:1024], start=True, stop=True), inc="t")
            tAT = C["t"]
            wts = [("t", t_r5)] if (c == 0 and t == 0) else []
            opx("vector", lambda e, t=t, p4=p4, xt=xt: e.tensor_scalar(
                s_sb[:], xt[:, t, :], mv_all[:, p4, t, 0:1], r1_all[:, p4, t, :],
                ALU.subtract, ALU.mult), waits=wts, inc="v")
            em.selfsync("vector", "v")
            opx("vector", lambda e: e.tensor_add(
                s_sb[:, 0:512], ps_h1a[:, 0:512], s_sb[:, 0:512]),
                waits=[("t", tAT)])
            opx("vector", lambda e: e.tensor_add(
                s_sb[:, 512:1024], ps_h1b[:, 0:512], s_sb[:, 512:1024]), inc="v")
            vS = C["v"]
            if t == 0 and do_ln:
                ln_vW = emit_ln_ph2(c + 2, ln_sA)
                ln_sB = emit_ln_sqrt2(c + 2, ln_vW)
                emit_ln_ph3(c + 2, ln_sB)
            for k in range(KH):
                opx("tensor", lambda e, k=k, t=t: e.matmul(
                    ps_ga[:, 0:512], h1_bf[:, k, t * 128:(t + 1) * 128],
                    w2T[:, k, 0:512], start=(k == 0), stop=(k == KH - 1)),
                    waits=[("v", PP["g_free"]), ("s", gelu_done[KH - 1])]
                    if k == 0 else ())
                opx("tensor", lambda e, k=k, t=t: e.matmul(
                    ps_gb[:, 0:512], h1_bf[:, k, t * 128:(t + 1) * 128],
                    w2T[:, k, 512:1024], start=(k == 0), stop=(k == KH - 1)),
                    inc="t" if k == KH - 1 else None)
            tG = C["t"]
            opx("vector", lambda e: e.tensor_add(
                g_sb[:, 0:512], ps_ga[:, 0:512], b2_bc[:, 0:512]),
                waits=[("t", tG)])
            opx("vector", lambda e: e.tensor_add(
                g_sb[:, 512:1024], ps_gb[:, 0:512], b2_bc[:, 512:1024]), inc="v")
            vG = C["v"]
            PP["g_free"] = vG
            em.selfsync("vector", "v")
            opx("vector", lambda e: e.tensor_mul(s_sb[:], s_sb[:], g_sb[:]),
                inc="v")
            vM1 = C["v"]
            opx("scalar", lambda e: e.activation(g_sb[:], g_sb[:], AF.Sigmoid),
                waits=[("v", vM1)], inc="s")
            sSIL = C["s"]
            opx("vector", lambda e: e.tensor_mul(s_sb[:], s_sb[:], g_sb[:]),
                waits=[("s", sSIL)], inc="v")
            em.selfsync("vector", "v")
            opx("vector", lambda e, t=t, xt=xt: e.tensor_add(
                xt[:, t, :], xt[:, t, :], s_sb[:]), inc="v")
            em.selfsync("vector", "v")
        vOUT = C["v"]
        opx("gpsimd", lambda e, c=c, xt=xt: e.dma_start(
            out_d[c * T:(c + 1) * T, :].rearrange("(t p) d -> p t d", p=128),
            xt[:]),
            waits=[("v", vOUT)], inc=("st", "su", "sv")[c % 3], inc_n=16)
        PP["tZ"] = tZ
        PP["sZ"] = sZ
        PP["vB"] = vB
        PP["vScan"] = vScan
        PP["vS_last"] = vS

    for snm in ("st", "su", "sv"):
        fv = C[snm]
        em.ops["sync"].append(lambda e, snm=snm, fv=fv: e.wait_ge(em.sems[snm], fv))

    with nc.Block() as block:

        @block.sync
        def _(e):
            for f in em.ops["sync"]:
                f(e)

        @block.gpsimd
        def _(e):
            for f in em.ops["gpsimd"]:
                f(e)

        @block.vector
        def _(e):
            for f in em.ops["vector"]:
                f(e)

        @block.scalar
        def _(e):
            for f in em.ops["scalar"]:
                f(e)

        @block.tensor
        def _(e):
            for f in em.ops["tensor"]:
                f(e)

    return nc


def kernel(L0_attn_x, L0_attn_W_a, L0_attn_b_a, L0_attn_W_in, L0_attn_b_in,
           L0_attn_W_out, L0_attn_b_out, L0_attn_ln_gamma, L0_attn_ln_beta,
           L0_ffn_w1, L0_ffn_b1, L0_ffn_w2, L0_ffn_b2):
    f = lambda a: np.ascontiguousarray(np.asarray(a, dtype=np.float32))
    x = f(L0_attn_x)
    shared = {
        "wa": f(L0_attn_W_a),
        "wi": f(L0_attn_W_in),
        "wo": f(L0_attn_W_out),
        "ba": f(L0_attn_b_a).reshape(1, HN),
        "bi": f(L0_attn_b_in).reshape(1, HN),
        "bo": f(L0_attn_b_out).reshape(1, D),
        "gm": f(L0_attn_ln_gamma).reshape(1, D),
        "bt": f(L0_attn_ln_beta).reshape(1, D),
        "w1": f(L0_ffn_w1),
        "b1v": f(L0_ffn_b1).reshape(KH, 128),
        "w2": f(L0_ffn_w2),
        "b2v": f(L0_ffn_b2).reshape(1, D),
    }
    nc = build()
    in_maps = [
        {k: np.copy(v) for k, v in dict(shared, x=x[b]).items()} for b in range(8)
    ]
    res = run_bass_kernel_spmd(nc, in_maps, core_ids=list(range(8)))
    return np.stack([res.results[b]["out"] for b in range(8)], axis=0)


# version 19 (final)
# speedup vs baseline: 1.2059x; 1.0031x over previous
import numpy as np
import concourse.bass as bass
import concourse.mybir as mybir
from concourse.bass_utils import run_bass_kernel_spmd

F32 = mybir.dt.float32
BF16 = mybir.dt.bfloat16
AF = mybir.ActivationFunctionType
ALU = mybir.AluOpType
AX = mybir.AxisListType

D = 1024
S = 4096
HN = 32
FF = 4096
T = 256
NT = T // 128
NCH = S // T
KD = D // 128
KH = FF // 128
EPS = 1e-5


class Emitter:
    def __init__(self, nc):
        self.nc = nc
        self.ops = {k: [] for k in ("sync", "gpsimd", "vector", "scalar", "tensor")}
        self.cnt = {k: 0 for k in ("ld", "wl", "wd", "we", "w2", "w3", "w4", "w5", "w6", "w7", "le", "lf", "st", "su", "sv", "v", "s", "t", "g")}
        self.sems = {k: nc.alloc_semaphore("sem_" + k) for k in self.cnt}

    def op(self, eng, fn, inc=None, inc_n=1, waits=()):
        for sname, val in waits:
            assert val <= self.cnt[sname], (eng, sname, val, self.cnt[sname])
        if inc is not None:
            self.cnt[inc] += inc_n
        sems = self.sems
        isem = sems[inc] if inc is not None else None

        def run(e, fn=fn, waits=tuple(waits), isem=isem, n=inc_n):
            for sname, val in waits:
                e.wait_ge(sems[sname], val)
            ins = fn(e)
            if isem is not None:
                ins.then_inc(isem, n)

        self.ops[eng].append(run)
        return self.cnt[inc] if inc is not None else None

    def selfsync(self, eng, sem):
        val = self.cnt[sem]
        sems = self.sems

        def run(e, val=val, sem=sem):
            e.wait_ge(sems[sem], val)

        self.ops[eng].append(run)


def build():
    nc = bass.Bass("TRN2", target_bir_lowering=False)

    x_d = nc.dram_tensor("x", [S, D], F32, kind="ExternalInput")
    wa_d = nc.dram_tensor("wa", [HN, D], F32, kind="ExternalInput")
    wi_d = nc.dram_tensor("wi", [HN, D], F32, kind="ExternalInput")
    wo_d = nc.dram_tensor("wo", [D, HN], F32, kind="ExternalInput")
    ba_d = nc.dram_tensor("ba", [1, HN], F32, kind="ExternalInput")
    bi_d = nc.dram_tensor("bi", [1, HN], F32, kind="ExternalInput")
    bo_d = nc.dram_tensor("bo", [1, D], F32, kind="ExternalInput")
    gm_d = nc.dram_tensor("gm", [1, D], F32, kind="ExternalInput")
    bt_d = nc.dram_tensor("bt", [1, D], F32, kind="ExternalInput")
    w1_d = nc.dram_tensor("w1", [FF, D], F32, kind="ExternalInput")
    b1_d = nc.dram_tensor("b1v", [KH, 128], F32, kind="ExternalInput")
    w2_d = nc.dram_tensor("w2", [D, FF], F32, kind="ExternalInput")
    b2_d = nc.dram_tensor("b2v", [1, D], F32, kind="ExternalInput")
    out_d = nc.dram_tensor("out", [S, D], F32, kind="ExternalOutput")

    w1T = nc.alloc_sbuf_tensor("w1T", [128, KD, KH, 128], BF16)
    w2T = nc.alloc_sbuf_tensor("w2T", [128, KH, 1024], BF16)
    waT_bf = nc.alloc_sbuf_tensor("waT_bf", [128, KD, HN], BF16)
    winT_bf = nc.alloc_sbuf_tensor("winT_bf", [128, KD, HN], BF16)
    woutT_bf = nc.alloc_sbuf_tensor("woutT_bf", [HN + 1, D], BF16)
    gamma_f = nc.alloc_sbuf_tensor("gamma_f", [128, KD], F32)
    beta_f = nc.alloc_sbuf_tensor("beta_f", [128, KD], F32)
    b1_col = nc.alloc_sbuf_tensor("b1_col", [128, KH], F32)
    ba_col = nc.alloc_sbuf_tensor("ba_col", [HN, 2], F32)
    bab_p = nc.alloc_sbuf_tensor("bab_p", [HN, 2], F32)
    b2_bc = nc.alloc_sbuf_tensor("b2_bc", [128, D], F32)
    ident_b = nc.alloc_sbuf_tensor("ident_b", [128, 128], BF16)
    ones_f = nc.alloc_sbuf_tensor("ones_f", [1, 128], F32)

    x_sbA = nc.alloc_sbuf_tensor("x_sbA", [128, NT, D], F32)
    x_sbB = nc.alloc_sbuf_tensor("x_sbB", [128, NT, D], F32)
    x_sbC = nc.alloc_sbuf_tensor("x_sbC", [128, NT, D], F32)
    xnf_bfA = nc.alloc_sbuf_tensor("xnf_bfA", [128, KD, T], BF16)
    xnf_bfB = nc.alloc_sbuf_tensor("xnf_bfB", [128, KD, T], BF16)
    uf_bf = nc.alloc_sbuf_tensor("uf_bf", [128, KD, T], BF16)
    h1_bf = nc.alloc_sbuf_tensor("h1_bf", [128, KH, T], BF16)
    g_sb = nc.alloc_sbuf_tensor("g_sb", [128, D], F32)
    s_sb = nc.alloc_sbuf_tensor("s_sb", [128, D], F32)
    stage = nc.alloc_sbuf_tensor("stage", [128, 4 * D], BF16)
    a_s = nc.alloc_sbuf_tensor("a_s", [HN, T], F32)
    b_s = nc.alloc_sbuf_tensor("b_s", [HN, T], F32)
    h_s = nc.alloc_sbuf_tensor("h_s", [HN, T], F32)
    h_bf = nc.alloc_sbuf_tensor("h_bf", [HN + 1, T], BF16)
    carry = nc.alloc_sbuf_tensor("carry", [HN, 1], F32)
    st6 = nc.alloc_sbuf_tensor("st6", [128, 12], F32)
    mv_all = nc.alloc_sbuf_tensor("mv_all", [128, 4, NT, 2], F32)
    r1_all = nc.alloc_sbuf_tensor("r1_all", [128, 4, NT, 1], F32)
    r12_all = nc.alloc_sbuf_tensor("r12_all", [128, NT], F32)
    veps = nc.alloc_sbuf_tensor("veps", [128, NT], F32)
    stdt = nc.alloc_sbuf_tensor("stdt", [128, 2 * NT], F32)
    w2t = nc.alloc_sbuf_tensor("w2t", [128, NT], F32)
    pw = veps
    alph = nc.alloc_sbuf_tensor("alph", [128, 6], F32)

    h1_flat = h1_bf[:].rearrange("p a b -> p (a b)")
    wstg = lambda a, b: h1_flat[:, a:b]
    x_flat = x_sbA[:].rearrange("p a b -> p (a b)")
    wa_st = lambda lo, j0, j1: x_flat[lo:lo + HN, j0:j1]
    waT_f = s_sb[:][:, 0:256].rearrange("p (a b) -> p a b", b=HN)
    winT_f = s_sb[:][:, 256:512].rearrange("p (a b) -> p a b", b=HN)
    wo_sb = g_sb[:][:, 0:256].rearrange("p (a b) -> p a b", b=HN)
    xB_flat = x_sbB[:].rearrange("p a b -> p (a b)")
    rows_sb = xB_flat[0:1, 0:D]
    xC_flat = x_sbC[:].rearrange("p a b -> p (a b)")
    ident_f = xC_flat[:, 0:128]
    b1v_sb = xC_flat[0:HN, 128:256]
    bab_sb = xC_flat[0:1, 256:256 + 2 * HN]

    ps_tr0 = nc.alloc_psum_tensor("ps_tr0", [128, 1024], BF16)
    ps_tr1 = nc.alloc_psum_tensor("ps_tr1", [128, 1024], BF16)
    ps_zab = nc.alloc_psum_tensor("ps_zab", [HN, 512], F32)
    ps_h1a = nc.alloc_psum_tensor("ps_h1a", [128, 512], F32)
    ps_h1b = nc.alloc_psum_tensor("ps_h1b", [128, 512], F32)
    ps_ga = nc.alloc_psum_tensor("ps_ga", [128, 512], F32)
    ps_gb = nc.alloc_psum_tensor("ps_gb", [128, 512], F32)
    ps_ms = nc.alloc_psum_tensor("ps_ms", [128, 512], F32)

    em = Emitter(nc)
    opx = em.op
    C = em.cnt

    opx("gpsimd", lambda e: e.memset(ident_f, 0.0), inc="g")
    g_if0 = C["g"]
    opx("gpsimd", lambda e: e.affine_select(
        out=ident_f, in_=ident_f, compare_op=ALU.not_equal, fill=1.0,
        base=0, pattern=[[-1, 128]], channel_multiplier=1),
        inc="g", waits=[("g", g_if0)])
    opx("gpsimd", lambda e: e.memset(ident_b[:], 0.0), inc="g")
    g_ib0 = C["g"]
    opx("gpsimd", lambda e: e.affine_select(
        out=ident_b[:], in_=ident_b[:], compare_op=ALU.not_equal, fill=1.0,
        base=0, pattern=[[-1, 128]], channel_multiplier=1),
        inc="g", waits=[("g", g_ib0)])
    opx("gpsimd", lambda e: e.memset(ones_f[:], 1.0), inc="g")
    opx("gpsimd", lambda e: e.memset(h_bf[HN:HN + 1, :], 1.0), inc="g")
    g_setup = C["g"]

    opx("gpsimd", lambda e: e.dma_start(wa_st(0, 0, D), wa_d[:]), inc="wl", inc_n=16)
    opx("gpsimd", lambda e: e.dma_start(
        wo_sb, wo_d[:].rearrange("(a p) c -> p a c", p=128)), inc="wl", inc_n=16)
    opx("gpsimd", lambda e: e.dma_start(bab_sb[0:1, 0:HN], ba_d[:]), inc="wl", inc_n=16)
    opx("gpsimd", lambda e: e.dma_start(bab_sb[0:1, HN:2 * HN], bi_d[:]), inc="wl", inc_n=16)
    opx("gpsimd", lambda e: e.dma_start(rows_sb, gm_d[:]), inc="wl", inc_n=16)
    opx("gpsimd", lambda e: e.dma_start(b1v_sb, b1_d[:]), inc="wl", inc_n=16)
    opx("gpsimd", lambda e: e.dma_start(woutT_bf[HN:HN + 1, :], bo_d[:]), inc="wl", inc_n=16)
    wl_small = C["wl"]

    opx("vector", lambda e: e.memset(carry[:], 0.0), inc="v")

    for j in range(KD):
        opx("tensor", lambda e, j=j: e.transpose(
            ps_ms[:, j * HN:(j + 1) * HN], wa_st(0, j * 128, (j + 1) * 128),
            ident_f[0:HN, 0:HN]),
            waits=[("wl", wl_small), ("g", g_setup)] if j == 0 else (),
            inc="t" if j == KD - 1 else None)
    t_r1a = C["t"]
    opx("vector", lambda e: e.tensor_copy(
        waT_f, ps_ms[:, 0:256].rearrange("p (a b) -> p a b", b=HN)),
        waits=[("t", t_r1a)], inc="v")
    v_r1a = C["v"]
    opx("gpsimd", lambda e: e.dma_start(wa_st(0, 0, D), wi_d[:]),
        inc="wl", inc_n=16, waits=[("t", t_r1a)])
    wl_wi = C["wl"]
    for j in range(KD):
        opx("tensor", lambda e, j=j: e.transpose(
            ps_ms[:, 256 + j * HN:256 + (j + 1) * HN],
            wa_st(0, j * 128, (j + 1) * 128), ident_f[0:HN, 0:HN]),
            waits=[("wl", wl_wi)] if j == 0 else (),
            inc="t" if j == KD - 1 else None)
    t_r1 = C["t"]
    opx("vector", lambda e: e.tensor_copy(
        winT_f, ps_ms[:, 256:512].rearrange("p (a b) -> p a b", b=HN)),
        waits=[("t", t_r1)], inc="v")
    v_r1 = C["v"]

    for j in range(KD):
        opx("tensor", lambda e, j=j: e.transpose(
            ps_ms[:, j:j + 1], rows_sb[:, j * 128:(j + 1) * 128],
            ident_f[0:1, 0:1]),
            waits=[("v", v_r1)] if j == 0 else ())
    opx("tensor", lambda e: e.transpose(
        ps_ms[:, 16:48], b1v_sb, ident_f[0:HN, 0:HN]))
    opx("tensor", lambda e: e.transpose(
        ps_ms[0:HN, 48:49], bab_sb[0:1, 0:HN], ident_f[0:1, 0:1]))
    opx("tensor", lambda e: e.transpose(
        ps_ms[0:HN, 49:50], bab_sb[0:1, HN:2 * HN], ident_f[0:1, 0:1]), inc="t")
    t_r2a = C["t"]
    opx("vector", lambda e: e.tensor_copy(gamma_f[:], ps_ms[:, 0:8]),
        waits=[("t", t_r2a)])
    opx("vector", lambda e: e.tensor_copy(b1_col[:], ps_ms[:, 16:48]))
    opx("vector", lambda e: e.tensor_copy(ba_col[:], ps_ms[0:HN, 48:50]), inc="v")
    v_r2a = C["v"]
    em.selfsync("vector", "v")
    opx("gpsimd", lambda e: e.dma_start(rows_sb, bt_d[:]),
        inc="wl", inc_n=16, waits=[("t", t_r2a)])
    wl_bt = C["wl"]
    for j in range(KD):
        opx("tensor", lambda e, j=j: e.transpose(
            ps_ms[:, 8 + j:9 + j], rows_sb[:, j * 128:(j + 1) * 128],
            ident_f[0:1, 0:1]),
            waits=[("wl", wl_bt), ("v", v_r2a)] if j == 0 else (),
            inc="t" if j == KD - 1 else None)
    t_r2b = C["t"]
    opx("vector", lambda e: e.tensor_copy(beta_f[:], ps_ms[:, 8:16]),
        waits=[("t", t_r2b)], inc="v")
    v_r2 = C["v"]
    em.selfsync("vector", "v")
    opx("gpsimd", lambda e: e.dma_start(rows_sb, b2_d[:]),
        inc="wl", inc_n=16, waits=[("t", t_r2b)])
    wl_b2 = C["wl"]

    v_evac_r3 = v_r2
    for h in range(2):
        for j in range(4):
            a = h * 4 + j
            opx("tensor", lambda e, a=a, j=j: e.transpose(
                ps_ms[0:HN, j * 128:(j + 1) * 128], wo_sb[:, a, :], ident_f),
                waits=[("v", v_evac_r3)] if j == 0 else (),
                inc="t" if j == 3 else None)
        t_r3 = C["t"]
        opx("vector", lambda e, h=h: e.tensor_copy(
            woutT_bf[0:HN, h * 512:(h + 1) * 512], ps_ms[0:HN, 0:512]),
            waits=[("t", t_r3)], inc="v")
        v_evac_r3 = C["v"]
    v_r3 = C["v"]

    for j in range(KD):
        opx("vector", lambda e, j=j: e.tensor_scalar_mul(
            waT_bf[:, j, :], waT_f[:, j, :], gamma_f[:, j:j + 1]))
    for j in range(KD):
        opx("vector", lambda e, j=j: e.tensor_scalar_mul(
            winT_bf[:, j, :], winT_f[:, j, :], gamma_f[:, j:j + 1]),
            inc="v" if j == KD - 1 else None)
    v_r4 = C["v"]

    for j in range(KD):
        opx("tensor", lambda e, j=j: e.matmul(
            ps_ms[0:HN, 0:1], waT_f[:, j, :], beta_f[:, j:j + 1],
            start=(j == 0), stop=(j == KD - 1)),
            waits=[("v", v_r3)] if j == 0 else ())
    for j in range(KD):
        opx("tensor", lambda e, j=j: e.matmul(
            ps_ms[0:HN, 1:2], winT_f[:, j, :], beta_f[:, j:j + 1],
            start=(j == 0), stop=(j == KD - 1)),
            inc="t" if j == KD - 1 else None)
    t_r5 = C["t"]
    opx("vector", lambda e: e.tensor_add(bab_p[:], ps_ms[0:HN, 0:2], ba_col[:]),
        waits=[("t", t_r5)], inc="v")
    v_r5 = C["v"]

    opx("tensor", lambda e: e.matmul(
        ps_ms[:, 0:512], ones_f[0:1, :], rows_sb[:, 0:512],
        start=True, stop=True),
        waits=[("v", v_r5), ("wl", wl_b2)], inc="t")
    t_b2a = C["t"]
    opx("vector", lambda e: e.tensor_copy(b2_bc[:, 0:512], ps_ms[:, 0:512]),
        waits=[("t", t_b2a)], inc="v")
    v_b2a = C["v"]
    opx("tensor", lambda e: e.matmul(
        ps_ms[:, 0:512], ones_f[0:1, :], rows_sb[:, 512:1024],
        start=True, stop=True),
        waits=[("v", v_b2a)], inc="t")
    t_b2b = C["t"]
    opx("vector", lambda e: e.tensor_copy(b2_bc[:, 512:1024], ps_ms[:, 0:512]),
        waits=[("t", t_b2b)], inc="v")
    v_b2_done = C["v"]


    ps_tr = [ps_tr0, ps_tr1]
    tr_free = {0: ("v", 0), 1: ("v", 0)}
    tr_fill = {0: 0, 1: 0}

    def trans8(bank, src_slices, dst_ap_fn, extra_waits, evac_eng="vector"):
        ps = ps_tr[bank]
        wts = list(extra_waits) + [tr_free[bank]]
        for k, src in enumerate(src_slices):
            opx("tensor", lambda e, k=k, src=src, ps=ps: e.transpose(
                ps[:, k * 128:(k + 1) * 128], src, ident_b[:]),
                waits=wts if k == 0 else (), inc="t" if k == len(src_slices) - 1 else None)
        tr_fill[bank] = C["t"]
        if evac_eng == "vector":
            opx("vector", lambda e, ps=ps: e.tensor_copy(
                dst_ap_fn(), ps[:].rearrange("p (a b) -> p a b", b=128)),
                waits=[("t", tr_fill[bank])], inc="v")
            tr_free[bank] = ("v", C["v"])
        else:
            opx("scalar", lambda e, ps=ps: e.copy(
                dst_ap_fn(), ps[:].rearrange("p (a b) -> p a b", b=128)),
                waits=[("t", tr_fill[bank])], inc="s")
            tr_free[bank] = ("s", C["s"])
        return tr_fill[bank]

    w1_fill = [0] * KH
    wsems = ["wd", "we", "w2", "w3", "w4", "w5", "w6", "w7"]

    def w1_part(i):
        o = (i % 8) * 1024
        a = (i % 3) * 2
        psem = wsems[i % 8]
        wl_i = C[psem]
        opx("vector", lambda e, o=o, a=a: e.reduce_sum(
            alph[:, a:a + 1], wstg(o, o + 1024), AX.X, apply_absolute_value=True),
            waits=[(psem, wl_i)], inc="v")
        em.selfsync("vector", "v")
        opx("vector", lambda e, a=a: e.tensor_scalar_mul(
            alph[:, a + 1:a + 2], alph[:, a:a + 1], 1.0 / D), inc="v")
        v_alpha = C["v"]
        em.selfsync("vector", "v")
        opx("scalar", lambda e, o=o: e.sign(wstg(o, o + 1024), wstg(o, o + 1024)),
            waits=[("v", v_alpha)], inc="s")
        return C["s"]

    def w1_tail(i, s_sign):
        o = (i % 8) * 1024
        a = (i % 3) * 2
        opx("vector", lambda e, o=o, a=a: e.tensor_scalar_mul(
            wstg(o, o + 1024), wstg(o, o + 1024), alph[:, a + 1:a + 2]),
            waits=[("s", s_sign)], inc="v")
        v_fold = C["v"]
        srcs = [wstg(o + k * 128, o + (k + 1) * 128) for k in range(KD)]
        w1_fill[i] = trans8(
            i % 2, srcs, (lambda i=i: w1T[:, :, i, :]), [("v", v_fold)],
            evac_eng="vector" if i % 3 == 0 else "scalar")

    w1_sign = {}
    for i in range(KH):
        wts = [("t", w1_fill[i - 8])] if i >= 8 else []
        psem = wsems[i % 8]
        opx("gpsimd", lambda e, i=i: e.dma_start(
            wstg((i % 8) * 1024, (i % 8) * 1024 + 1024),
            w1_d[i * 128:(i + 1) * 128, :]),
            inc=psem, inc_n=16, waits=wts)
        w1_sign[i] = w1_part(i)
        if i >= 1:
            w1_tail(i - 1, w1_sign[i - 1])
    w1_tail(KH - 1, w1_sign[KH - 1])

    w2_fill = [0] * KD
    w2_sign = {}

    def w2_round(i, r):
        o = (i % 2) * 4096
        srcs = [wstg(o + (r * 8 + j) * 128, o + (r * 8 + j + 1) * 128)
                for j in range(8)]
        w2_fill[i] = trans8(
            r % 2, srcs,
            (lambda i=i, r=r: w2T[:, r * 8:(r + 1) * 8, i * 128:(i + 1) * 128]),
            [("v", w2_fold[i])] if r == 0 else [],
            evac_eng="scalar")

    w2_fold = {}
    for i in range(KD):
        o = (i % 2) * 4096
        if i == 0:
            wts = [("t", w1_fill[27])]
        elif i == 1:
            wts = [("t", w1_fill[31])]
        else:
            wts = [("t", w2_fill[i - 2])]
        psem = wsems[i % 2]
        opx("gpsimd", lambda e, i=i, o=o: e.dma_start(
            wstg(o, o + 4096), w2_d[i * 128:(i + 1) * 128, :]),
            inc=psem, inc_n=16, waits=wts)
        wl_i = C[psem]
        for r in range(4):
            opx("vector", lambda e, o=o, r=r: e.reduce_sum(
                stdt[:, r:r + 1], wstg(o + r * 1024, o + (r + 1) * 1024),
                AX.X, apply_absolute_value=True),
                waits=[(psem, wl_i)] if r == 0 else (),
                inc="v" if r == 3 else None)
            if i >= 1:
                w2_round(i - 1, r)
        em.selfsync("vector", "v")
        a = (i % 3) * 2
        opx("vector", lambda e, a=a: e.tensor_add(
            alph[:, a:a + 1], stdt[:, 0:1], stdt[:, 1:2]))
        opx("vector", lambda e: e.tensor_add(
            stdt[:, 2:3], stdt[:, 2:3], stdt[:, 3:4]), inc="v")
        em.selfsync("vector", "v")
        opx("vector", lambda e, a=a: e.tensor_add(
            alph[:, a:a + 1], alph[:, a:a + 1], stdt[:, 2:3]), inc="v")
        em.selfsync("vector", "v")
        opx("vector", lambda e, a=a: e.tensor_scalar_mul(
            alph[:, a + 1:a + 2], alph[:, a:a + 1], 1.0 / FF), inc="v")
        v_alpha = C["v"]
        em.selfsync("vector", "v")
        opx("scalar", lambda e, o=o: e.sign(wstg(o, o + 4096), wstg(o, o + 4096)),
            waits=[("v", v_alpha)], inc="s")
        s_sign = C["s"]
        opx("vector", lambda e, o=o, a=a: e.tensor_scalar_mul(
            wstg(o, o + 4096), wstg(o, o + 4096), alph[:, a + 1:a + 2]),
            waits=[("s", s_sign)], inc="v")
        w2_fold[i] = C["v"]
    for r in range(4):
        w2_round(KD - 1, r)

    prologue_end = dict(C)

    x_par = [x_sbA, x_sbB, x_sbC]
    ldsem = ["ld", "le", "lf"]
    xload_cnt = {}
    stage_free = {j: 0 for j in range(4)}
    PP = dict(vU=[0, 0], vXN=[0, 0], tZ=0, sZ=0, vB=0, vScan=0,
              gelu_tail=[0, 0, 0], g_free=0, lnS=0, uf_ev=("v", 0),
              xnf_ev=("v", 0), vS_last=0)
    xnf_par = [xnf_bfA, xnf_bfB]

    def emit_trans_group(cc, slot):
        t = slot // 2
        is_xn = slot % 2
        pcc = cc % 2
        if is_xn:
            stage_free[slot] = trans8(
                1, [stage[:, slot * D + k * 128:slot * D + (k + 1) * 128]
                    for k in range(KD)],
                (lambda t=t, pcc=pcc: xnf_par[pcc][:, :, t * 128:(t + 1) * 128]),
                [("v", PP["vXN"][t])])
            PP["xnf_ev"] = tr_free[1]
        else:
            stage_free[slot] = trans8(
                0, [stage[:, slot * D + k * 128:slot * D + (k + 1) * 128]
                    for k in range(KD)],
                (lambda t=t: uf_bf[:, :, t * 128:(t + 1) * 128]),
                [("v", PP["vU"][t])])
            PP["uf_ev"] = tr_free[0]
    gelu_done = [0] * KH

    def emit_xload(c):
        p = c % 3
        if c == 0:
            wts = [("t", t_r1)]
        elif c == 1:
            wts = [("t", t_b2b)]
        elif c == 2:
            wts = [("t", t_r3)]
        else:
            wts = [(("st", "su", "sv")[c % 3], 16 * ((c - 3) // 3 + 1))]
        opx("sync", lambda e, c=c, p=p: e.dma_start(
            x_par[p][:], x_d[c * T:(c + 1) * T, :].rearrange(
                "(t p) d -> p t d", p=128)),
            inc=ldsem[p], inc_n=16, waits=wts)
        xload_cnt[c] = C[ldsem[p]]

    def emit_ln_ph1(cc):
        p = cc % 4
        xt = x_par[cc % 3]
        vA = [0, 0]
        for t in range(NT):
            wts = []
            if t == 0:
                wts = [(ldsem[cc % 3], xload_cnt[cc]), ("s", PP["lnS"])]
            opx("vector", lambda e, t=t, xt=xt: e.bn_stats(
                st6[:, 0:6], xt[:, t, 0:512]), waits=wts)
            opx("vector", lambda e, t=t, xt=xt: e.bn_stats(
                st6[:, 6:12], xt[:, t, 512:1024]), inc="v")
            em.selfsync("vector", "v")
            opx("vector", lambda e, t=t, p=p: e.bn_aggr(
                mv_all[:, p, t, :], st6[:]), inc="v")
            em.selfsync("vector", "v")
            opx("vector", lambda e, t=t, p=p: e.tensor_scalar_add(
                veps[:, t:t + 1], mv_all[:, p, t, 1:2], EPS), inc="v")
            vA[t] = C["v"]
        return vA

    def emit_ln_sqrt1(cc, vA):
        sA = [0, 0]
        for t in range(NT):
            opx("scalar", lambda e, t=t: e.sqrt(
                stdt[:, t:t + 1], veps[:, t:t + 1]),
                waits=[("v", vA[t])], inc="s")
            sA[t] = C["s"]
        return sA

    def emit_ln_ph2(cc, sA):
        p = cc % 4
        vW = [0, 0]
        for t in range(NT):
            opx("vector", lambda e, t=t, p=p: e.reciprocal(
                r1_all[:, p, t, :], stdt[:, t:t + 1]),
                waits=[("s", sA[t])], inc="v")
            em.selfsync("vector", "v")
            opx("vector", lambda e, t=t, p=p: e.tensor_mul(
                pw[:, t:t + 1], r1_all[:, p, t, :], r1_all[:, p, t, :]), inc="v")
            em.selfsync("vector", "v")
            opx("vector", lambda e, t=t, p=p: e.tensor_scalar(
                w2t[:, t:t + 1], pw[:, t:t + 1], mv_all[:, p, t, 1:2], EPS,
                ALU.mult, ALU.add), inc="v")
            vW[t] = C["v"]
        return vW

    def emit_ln_sqrt2(cc, vW):
        sB = [0, 0]
        for t in range(NT):
            opx("scalar", lambda e, t=t: e.sqrt(
                stdt[:, NT + t:NT + t + 1], w2t[:, t:t + 1]),
                waits=[("v", vW[t])], inc="s")
            sB[t] = C["s"]
        PP["lnS"] = C["s"]
        return sB

    def emit_ln_ph3(cc, sB):
        p = cc % 4
        xt = x_par[cc % 3]
        for t in range(NT):
            opx("vector", lambda e, t=t: e.reciprocal(
                pw[:, t:t + 1], stdt[:, NT + t:NT + t + 1]),
                waits=[("s", sB[t])], inc="v")
            em.selfsync("vector", "v")
            opx("vector", lambda e, t=t, p=p: e.tensor_mul(
                r12_all[:, t:t + 1], r1_all[:, p, t, :], pw[:, t:t + 1]), inc="v")
            em.selfsync("vector", "v")
            opx("vector", lambda e, t=t, p=p, xt=xt: e.tensor_scalar(
                stage[:, (2 * t) * D:(2 * t + 1) * D], xt[:, t, :],
                mv_all[:, p, t, 0:1], r12_all[:, t:t + 1],
                ALU.subtract, ALU.mult),
                waits=[("t", stage_free[2 * t])], inc="v")
            PP["vU"][t] = C["v"]
            opx("vector", lambda e, t=t, p=p, xt=xt: e.tensor_scalar(
                stage[:, (2 * t + 1) * D:(2 * t + 2) * D], xt[:, t, :],
                mv_all[:, p, t, 0:1], r1_all[:, p, t, :],
                ALU.subtract, ALU.mult),
                waits=[("t", stage_free[2 * t + 1])], inc="v")
            PP["vXN"][t] = C["v"]

    def emit_ln_block(cc):
        vA = emit_ln_ph1(cc)
        sA = emit_ln_sqrt1(cc, vA)
        vW = emit_ln_ph2(cc, sA)
        sB = emit_ln_sqrt2(cc, vW)
        emit_ln_ph3(cc, sB)

    emit_xload(0)
    emit_xload(1)
    emit_ln_block(0)
    for slot in range(4):
        emit_trans_group(0, slot)
    emit_ln_block(1)

    for c in range(NCH):
        p = c % 3
        xt = x_par[p]
        pf = c % 2
        p4 = c % 4
        if c + 2 < NCH:
            emit_xload(c + 2)

        uf_ev = PP["uf_ev"]
        xnf_ev = PP["xnf_ev"]
        xnf_c = xnf_par[pf]

        for k in range(KD):
            opx("tensor", lambda e, k=k: e.matmul(
                ps_zab[:, 0:T], waT_bf[:, k, :], uf_bf[:, k, :],
                start=(k == 0), stop=(k == KD - 1)),
                waits=[uf_ev, ("s", PP["sZ"]), ("v", PP["vB"])]
                if k == 0 else ())
        for k in range(KD):
            opx("tensor", lambda e, k=k: e.matmul(
                ps_zab[:, 256:256 + T], winT_bf[:, k, :], uf_bf[:, k, :],
                start=(k == 0), stop=(k == KD - 1)),
                inc="t" if k == KD - 1 else None)
        tZ = C["t"]

        opx("scalar", lambda e: e.activation(
            a_s[:], ps_zab[:, 0:T], AF.Sigmoid, bias=bab_p[:, 0:1]),
            waits=[("t", tZ), ("v", PP["vScan"])], inc="s")
        sZ = C["s"]
        opx("vector", lambda e: e.tensor_scalar_add(
            b_s[:], ps_zab[:, 256:256 + T], bab_p[:, 1:2]), waits=[("t", tZ)],
            inc="v")
        vB = C["v"]
        em.selfsync("vector", "v")
        opx("vector", lambda e: e.tensor_tensor_scan(
            h_s[:], a_s[:], b_s[:], carry[:, 0:1], ALU.mult, ALU.add),
            waits=[("s", sZ)], inc="v")
        vScan = C["v"]
        em.selfsync("vector", "v")
        opx("vector", lambda e: e.tensor_copy(carry[:], h_s[:, T - 1:T]), inc="v")
        em.selfsync("vector", "v")
        opx("vector", lambda e: e.tensor_copy(h_bf[0:HN, :], h_s[:]), inc="v")
        vH = C["v"]

        h1_banks = [ps_h1a, ps_h1b, ps_ms]
        for m in range(KH):
            if c + 1 < NCH and m in (20, 23, 26, 29):
                emit_trans_group(c + 1, (m - 20) // 3)
            ps = h1_banks[m % 3]
            if m == 0:
                wts = [xnf_ev, ("s", PP["gelu_tail"][0]),
                       ("v", PP["vS_last"])]
            elif m in (1, 2):
                wts = [("s", PP["gelu_tail"][m])]
                if m == 2 and c == 0:
                    wts.append(("v", v_b2_done))
            else:
                wts = [("s", gelu_done[m - 3])]
            for k in range(KD):
                opx("tensor", lambda e, k=k, m=m, ps=ps, xnf_c=xnf_c: e.matmul(
                    ps[:, 0:T], w1T[:, k, m, :], xnf_c[:, k, :],
                    start=(k == 0), stop=(k == KD - 1)),
                    waits=wts if k == 0 else (),
                    inc="t" if k == KD - 1 else None)
            tM = C["t"]
            opx("scalar", lambda e, m=m, ps=ps: e.activation(
                h1_bf[:, m, :], ps[:, 0:T], AF.Gelu, bias=b1_col[:, m:m + 1]),
                waits=[("t", tM)], inc="s")
            gelu_done[m] = C["s"]
        PP["gelu_tail"] = [gelu_done[30], gelu_done[31], gelu_done[29]]

        do_ln = c + 2 < NCH
        if do_ln:
            ln_vA = emit_ln_ph1(c + 2)
            ln_sA = emit_ln_sqrt1(c + 2, ln_vA)

        vS = 0
        for t in range(NT):
            awts = [("v", vH), ("s", gelu_done[KH - 1])] if t == 0 else [("v", vS)]
            opx("tensor", lambda e, t=t: e.matmul(
                ps_h1a[:, 0:512], h_bf[:, t * 128:(t + 1) * 128],
                woutT_bf[:, 0:512], start=True, stop=True),
                waits=awts)
            opx("tensor", lambda e, t=t: e.matmul(
                ps_h1b[:, 0:512], h_bf[:, t * 128:(t + 1) * 128],
                woutT_bf[:, 512:1024], start=True, stop=True), inc="t")
            tAT = C["t"]
            wts = [("t", t_r5)] if (c == 0 and t == 0) else []
            opx("vector", lambda e, t=t, p4=p4, xt=xt: e.tensor_scalar(
                s_sb[:], xt[:, t, :], mv_all[:, p4, t, 0:1], r1_all[:, p4, t, :],
                ALU.subtract, ALU.mult), waits=wts, inc="v")
            em.selfsync("vector", "v")
            opx("vector", lambda e: e.tensor_add(
                s_sb[:, 0:512], ps_h1a[:, 0:512], s_sb[:, 0:512]),
                waits=[("t", tAT)])
            opx("vector", lambda e: e.tensor_add(
                s_sb[:, 512:1024], ps_h1b[:, 0:512], s_sb[:, 512:1024]), inc="v")
            vS = C["v"]
            if t == 0 and do_ln:
                ln_vW = emit_ln_ph2(c + 2, ln_sA)
                ln_sB = emit_ln_sqrt2(c + 2, ln_vW)
                emit_ln_ph3(c + 2, ln_sB)
            for k in range(KH):
                opx("tensor", lambda e, k=k, t=t: e.matmul(
                    ps_ga[:, 0:512], h1_bf[:, k, t * 128:(t + 1) * 128],
                    w2T[:, k, 0:512], start=(k == 0), stop=(k == KH - 1)),
                    waits=[("v", PP["g_free"]), ("s", gelu_done[KH - 1])]
                    if k == 0 else ())
                opx("tensor", lambda e, k=k, t=t: e.matmul(
                    ps_gb[:, 0:512], h1_bf[:, k, t * 128:(t + 1) * 128],
                    w2T[:, k, 512:1024], start=(k == 0), stop=(k == KH - 1)),
                    inc="t" if k == KH - 1 else None)
            tG = C["t"]
            opx("vector", lambda e: e.tensor_add(
                g_sb[:, 0:512], ps_ga[:, 0:512], b2_bc[:, 0:512]),
                waits=[("t", tG)])
            opx("vector", lambda e: e.tensor_add(
                g_sb[:, 512:1024], ps_gb[:, 0:512], b2_bc[:, 512:1024]), inc="v")
            vG = C["v"]
            PP["g_free"] = vG
            em.selfsync("vector", "v")
            opx("vector", lambda e: e.tensor_mul(s_sb[:], s_sb[:], g_sb[:]),
                inc="v")
            vM1 = C["v"]
            opx("scalar", lambda e: e.activation(g_sb[:], g_sb[:], AF.Sigmoid),
                waits=[("v", vM1)], inc="s")
            sSIL = C["s"]
            opx("vector", lambda e: e.tensor_mul(s_sb[:], s_sb[:], g_sb[:]),
                waits=[("s", sSIL)], inc="v")
            em.selfsync("vector", "v")
            opx("vector", lambda e, t=t, xt=xt: e.tensor_add(
                xt[:, t, :], xt[:, t, :], s_sb[:]), inc="v")
            em.selfsync("vector", "v")
        vOUT = C["v"]
        opx("gpsimd", lambda e, c=c, xt=xt: e.dma_start(
            out_d[c * T:(c + 1) * T, :].rearrange("(t p) d -> p t d", p=128),
            xt[:]),
            waits=[("v", vOUT)], inc=("st", "su", "sv")[c % 3], inc_n=16)
        PP["tZ"] = tZ
        PP["sZ"] = sZ
        PP["vB"] = vB
        PP["vScan"] = vScan
        PP["vS_last"] = vS

    for snm in ("st", "su", "sv"):
        fv = C[snm]
        em.ops["sync"].append(lambda e, snm=snm, fv=fv: e.wait_ge(em.sems[snm], fv))

    with nc.Block() as block:

        @block.sync
        def _(e):
            for f in em.ops["sync"]:
                f(e)

        @block.gpsimd
        def _(e):
            for f in em.ops["gpsimd"]:
                f(e)

        @block.vector
        def _(e):
            for f in em.ops["vector"]:
                f(e)

        @block.scalar
        def _(e):
            for f in em.ops["scalar"]:
                f(e)

        @block.tensor
        def _(e):
            for f in em.ops["tensor"]:
                f(e)

    return nc


def kernel(L0_attn_x, L0_attn_W_a, L0_attn_b_a, L0_attn_W_in, L0_attn_b_in,
           L0_attn_W_out, L0_attn_b_out, L0_attn_ln_gamma, L0_attn_ln_beta,
           L0_ffn_w1, L0_ffn_b1, L0_ffn_w2, L0_ffn_b2):
    f = lambda a: np.ascontiguousarray(np.asarray(a, dtype=np.float32))
    x = f(L0_attn_x)
    shared = {
        "wa": f(L0_attn_W_a),
        "wi": f(L0_attn_W_in),
        "wo": f(L0_attn_W_out),
        "ba": f(L0_attn_b_a).reshape(1, HN),
        "bi": f(L0_attn_b_in).reshape(1, HN),
        "bo": f(L0_attn_b_out).reshape(1, D),
        "gm": f(L0_attn_ln_gamma).reshape(1, D),
        "bt": f(L0_attn_ln_beta).reshape(1, D),
        "w1": f(L0_ffn_w1),
        "b1v": f(L0_ffn_b1).reshape(KH, 128),
        "w2": f(L0_ffn_w2),
        "b2v": f(L0_ffn_b2).reshape(1, D),
    }
    nc = build()
    in_maps = [
        {k: np.copy(v) for k, v in dict(shared, x=x[b]).items()} for b in range(8)
    ]
    res = run_bass_kernel_spmd(nc, in_maps, core_ids=list(range(8)))
    return np.stack([res.results[b]["out"] for b in range(8)], axis=0)
